# version 01
# baseline (speedup 1.0000x reference)
import numpy as np
from contextlib import ExitStack

import concourse.bass as bass
import concourse.mybir as mybir
from concourse.bass_utils import run_bass_kernel_spmd

F32 = mybir.dt.float32
BF16 = mybir.dt.bfloat16
AF = mybir.ActivationFunctionType
ALU = mybir.AluOpType
AX = mybir.AxisListType.X

D = 1024
DI = 2048
T = 4096
NS = 16
NB = 512
HF = 4096
CH = 512
NCH = T // CH
NTT = CH // 128
KD = D // 128
KI = DI // 128
KH = HF // 128
KB = NB // 128
LN_EPS = 1e-5
TAU = 10.0
FEPS = 1e-6

ENGS = ("sync", "gpsimd", "vector", "scalar", "tensor")
GAP_WIDE = 2
GAP_NARROW = 4


def ts(i, n=128):
    return slice(i * n, (i + 1) * n)


class Buf:
    __slots__ = ("writer", "readers", "name")

    def __init__(self, name=""):
        self.writer = None
        self.readers = []
        self.name = name


class Prog:
    def __init__(self, nc, es):
        self.nc = nc
        self.ops = {e: [] for e in ENGS}
        self.sems = {e: es.enter_context(nc.semaphore(f"sem_{e}")) for e in ENGS}
        self.count = {e: 0 for e in ENGS}
        self.ord = {e: 0 for e in ENGS}
        self.waited = {e: set() for e in ENGS}

    def op(self, eng, fn, reads=(), writes=(), dma=False, w=0):
        rdeps = [b.writer for b in reads if b.writer is not None]
        wdeps = []
        for b in writes:
            if b.writer is not None:
                wdeps.append(b.writer)
            wdeps.extend(b.readers)
        inc = 16 if dma else 1
        pre = dict(self.count)
        self.count[eng] += inc
        self.ord[eng] += 1
        tok = (eng, self.count[eng], dma, self.ord[eng], w)
        wl = {}
        for (e, v, pdma, pord, pw) in rdeps:
            if pdma:
                vv = pre[e]
                if vv > wl.get(e, 0):
                    wl[e] = vv
            elif e != eng:
                if v > wl.get(e, 0):
                    wl[e] = v
            else:
                dist = self.ord[eng] - pord
                if dist < GAP_WIDE or (dist < GAP_NARROW and pw < 256):
                    if v > wl.get(e, 0):
                        wl[e] = v
        for (e, v, pdma, pord, pw) in wdeps:
            if pdma:
                vv = pre[e]
                if vv > wl.get(e, 0):
                    wl[e] = vv
            elif e != eng:
                if v > wl.get(e, 0):
                    wl[e] = v
            else:
                dist = self.ord[eng] - pord
                if dist < GAP_WIDE or (dist < GAP_NARROW and pw < 256):
                    if v > wl.get(e, 0):
                        wl[e] = v
        self.ops[eng].append((sorted(wl.items()), fn, dma))
        for (e, v) in wl.items():
            self.waited[e].add(v)
        for b in reads:
            b.readers.append(tok)
        for b in writes:
            b.writer = tok
            b.readers = []
        return tok

    def emit(self, block):
        sems = self.sems
        waited = self.waited

        def make_body(eng_name, eng_ops):
            def body(eng):
                hwm = {x: 0 for x in ENGS}
                pend = 0
                val = 0
                for waits, fn, isdma in eng_ops:
                    for (se, sv) in waits:
                        if sv > hwm[se]:
                            eng.wait_ge(sems[se], sv)
                            hwm[se] = sv
                    ins = fn(eng)
                    if isdma:
                        val += 16
                        ins.then_inc(sems[eng_name], 16)
                    else:
                        val += 1
                        pend += 1
                        if val in waited[eng_name]:
                            eng.drain(fusable=True).then_inc(sems[eng_name], pend)
                            pend = 0
            return body

        for e in ENGS:
            getattr(block, e)(make_body(e, self.ops[e]))


def build_program(debug_taps=()):
    nc = bass.Bass("TRN2", target_bir_lowering=False)
    es = ExitStack()

    def din(name, shape):
        return nc.dram_tensor(name, list(shape), F32, kind="ExternalInput").ap()

    X_in = din("x", (T, D))
    ln_w_d = din("ln_w", (D,))
    ln_b_d = din("ln_b", (D,))
    W_in_d = din("W_in", (D, 2 * DI))
    b_in_d = din("b_in", (2 * DI,))
    conv_w_d = din("conv_w", (DI, 4))
    conv_b_d = din("conv_b", (DI,))
    A_log_d = din("A_log", (DI, NS))
    D_vec_d = din("Dvec", (DI,))
    W_x_d = din("W_x", (DI, 2 * NS))
    b_x_d = din("b_x", (2 * NS,))
    W_out_d = din("W_out", (DI, D))
    b_out_d = din("b_out", (D,))
    W1_d = din("W1", (HF, D))
    W2_d = din("W2", (D, HF))
    ln_g_d = din("ln_g", (D,))
    ln_b4_d = din("ln_b4", (D,))
    router_W_d = din("router_W", (NB, D))
    router_b_d = din("router_b", (NB,))
    raw_U_d = din("raw_U", (NB, D))
    raw_V_d = din("raw_V", (NB, D))
    gamma_d = din("gamma", (D,))
    c_row_d = din("c_row", (D,))
    wmod_d = din("wmod", (D, 6 * D))

    OUT = nc.dram_tensor("out", [T, D], F32, kind="ExternalOutput").ap()

    taps = {}
    TAPS = {
        "tp_xconv": ([DI, T], BF16),
        "tp_ssm": ([32, T], BF16),
        "tp_ytot": ([DI, T], BF16),
        "tp_X1": ([T, D], F32),
        "tp_r": ([T, NB], F32),
        "tp_G": ([T, NB], F32),
        "tp_hff": ([D, T], BF16),
        "tp_xn2f": ([D, T], BF16),
        "tp_xmain": ([DI, T], BF16),
        "tp_gate": ([DI, T], BF16),
        "tp_modrow": ([6, D], F32),
        "tp_xn1": ([T, D], F32),
        "tp_h1": ([T, D], F32),
        "tp_xup": ([T, D], F32),
        "tp_H0": ([128, NS * CH], BF16),
        "tp_gar": ([128, D], BF16),
        "tp_gacol": ([128, KD], F32),
        "tp_prep": ([128, 6 * D], BF16),
        "tp_sel": ([32, 32 * 128], BF16),
        "tp_asb": ([128, KI * NS], F32),
        "tp_misc": ([128, 160], F32),
        "tp_brep": ([128, NS * CH], BF16),
        "tp_crep": ([128, NS * CH], BF16),
    }
    for name in debug_taps:
        shp, dt_ = TAPS[name]
        taps[name] = nc.dram_tensor(name, list(shp), dt_, kind="ExternalOutput").ap()

    Win_scr = nc.dram_tensor("Win_scr", [KD, 128, 8 * 512], BF16, kind="Internal").ap()
    Wout_scr = nc.dram_tensor("Wout_scr", [4, 128, KI * 256], BF16, kind="Internal").ap()
    W1_scr = nc.dram_tensor("W1_scr", [KD, 128, 8 * 512], BF16, kind="Internal").ap()
    W2_scr = nc.dram_tensor("W2_scr", [8, 128, 4 * 1024], BF16, kind="Internal").ap()
    V_scr = nc.dram_tensor("V_scr", [KB, 128, D], BF16, kind="Internal").ap()
    mod_scr = nc.dram_tensor("mod_scr", [2, D], F32, kind="Internal").ap()

    def asb(name, shape, dtype=F32):
        return es.enter_context(nc.sbuf_tensor(name, list(shape), dtype))

    ident32 = asb("ident32", [128, 128], F32)
    ident16 = asb("ident16", [128, 128], BF16)
    pidx = asb("pidx", [128, 128], F32)
    ones16 = asb("ones16", [1, 128], BF16)
    ones32 = asb("ones32", [1, 128], F32)
    SEL = asb("SEL", [32, 32 * 128], BF16)
    ca1_rep = asb("ca1_rep", [128, D], BF16)
    ga_rep = asb("ga_rep", [128, D], BF16)
    sa_rep = asb("sa_rep", [128, D], BF16)
    cb1_rep = asb("cb1_rep", [128, D], BF16)
    sb_rep = asb("sb_rep", [128, D], BF16)
    ga_col = asb("ga_col", [128, KD], F32)
    gb_col = asb("gb_col", [128, KD], F32)
    gabo_col = asb("gabo_col", [128, KD], F32)
    sgn8 = asb("sgn8", [128, KD], F32)
    lnw_col = asb("lnw_col", [128, KD], F32)
    lnb16 = asb("lnb16", [128, KD], BF16)
    lng_col = asb("lng_col", [128, KD], F32)
    lnb4_16 = asb("lnb4_16", [128, KD], BF16)
    bout_col = asb("bout_col", [128, KD], F32)
    btin_col = asb("btin_col", [128, 32], F32)
    bx_col = asb("bx_col", [32, 1], F32)
    btr_row = asb("btr_row", [1, NB], BF16)
    btr32 = asb("btr32", [1, NB], F32)
    A_sb = asb("A_sb", [128, KI * NS], F32)
    D_col = asb("D_col", [128, KI], F32)
    convw_col = asb("convw_col", [128, KI * 4], F32)
    convb_col = asb("convb_col", [128, KI], F32)
    carry = asb("carry", [128, KI * NS], BF16)
    ctail = asb("ctail", [128, KI * 3], BF16)
    silu_c = asb("silu_c", [128, KD], F32)
    csb = asb("csb", [128, KD], F32)
    Wx_sb = asb("Wx_sb", [128, KI * 32], BF16)
    uvn = asb("uvn", [128, 4], F32)
    lns = asb("lns", [128, 8], F32)
    tks = asb("tks", [128, 24], F32)
    srow = asb("srow", [1, 512], F32)
    constt = asb("constt", [128, 2], F32)
    ssm_bf = asb("ssm_bf", [32, CH], BF16)
    convacc = asb("convacc", [128, CH], F32)
    Zbuf = asb("Zbuf", [128, CH], BF16)
    yb = asb("yb", [128, CH], F32)
    rW_sb = asb("rW_sb", [128, KD * NB], BF16)
    U_sb = asb("U_sb", [128, KD * NB], BF16)

    x_tok = asb("x_tok", [128, NTT * D], F32)
    lnA = asb("lnA", [128, D], F32)
    lnB = asb("lnB", [128, D], F32)
    wbufA = asb("wbufA", [128, 4096], BF16)
    R1 = asb("R1", [128, KI * (3 + CH)], BF16)
    R2 = asb("R2", [128, KI * CH], BF16)
    R3 = asb("R3", [128, KI * CH], BF16)
    R4 = asb("R4", [128, 2 * NS * CH], BF16)
    R5 = asb("R5", [128, NS * CH], BF16)
    W_slab = asb("W_slab", [128, NS * CH], BF16)

    psum = [es.enter_context(nc.psum_tensor(f"ps{i}", [128, 512], F32)) for i in range(8)]

    lnsq = lnA
    gam_rep = lnA
    t1f = yb
    x_main = R1
    ytot = R1[:, 0:KI * CH]
    x2tok = R1.bitcast(F32)[:, 0:NTT * D]
    x_conv = R2
    X1tok = R2.bitcast(F32)[:, 0:NTT * D]
    gate_s = R3
    hff = R3[:, 0:KD * CH]
    xn4f = R3[:, KD * CH:2 * KD * CH]
    prep_stage = R4.bitcast(F32)[:, 0:4096]
    prep_bf = R4[:, 2 * 4096:3 * 4096]
    B_rep = R4[:, 0:NS * CH]
    C_rep = R4[:, NS * CH:]
    g1 = R4
    xn2f = R5[:, 0:KD * CH]
    H_slab = R5
    r_tok = R5.bitcast(F32)[:, 0:512]
    hlat = R5.bitcast(F32)[:, 512:1024]
    alphaf = R5.bitcast(F32)[:, 1024:1536]
    zap = R5.bitcast(F32)[:, 1536:2048]
    G_tok = R5.bitcast(F32)[:, 2048:2560]
    Gf = R5[:, 5120:5120 + KB * CH]

    P = Prog(nc, es)
    B = {}

    def buf(name):
        if name not in B:
            B[name] = Buf(name)
        return B[name]

    pb = [buf(f"ps{i}") for i in range(8)]

    def dma(dst, src, reads=(), writes=(), q="sync", slow=False):
        if slow:
            return P.op(q, lambda e, d=dst, s=src: e.dma_start(
                out=d, in_=s, allow_slow_non_contiguous=True),
                reads=list(reads), writes=list(writes), dma=True)
        return P.op(q, lambda e, d=dst, s=src: e.dma_start(out=d, in_=s),
                    reads=list(reads), writes=list(writes), dma=True)

    def vop(fn, reads=(), writes=(), w=0):
        return P.op("vector", fn, reads=list(reads), writes=list(writes), w=w)

    def sop(fn, reads=(), writes=(), w=0):
        return P.op("scalar", fn, reads=list(reads), writes=list(writes), w=w)

    def top(fn, reads=(), writes=()):
        return P.op("tensor", fn, reads=list(reads), writes=list(writes))

    def gop(fn, reads=(), writes=()):
        return P.op("gpsimd", fn, reads=list(reads), writes=list(writes))

    gop(lambda e: e.iota(ident32[:], pattern=[[1, 128]], base=0,
                         channel_multiplier=0, allow_small_or_imprecise_dtypes=True),
        writes=[buf("id")])
    gop(lambda e: e.iota(pidx[:], pattern=[[0, 128]], base=0,
                         channel_multiplier=1, allow_small_or_imprecise_dtypes=True),
        writes=[buf("pidx")])
    vop(lambda e: e.tensor_tensor(out=ident32[:], in0=ident32[:], in1=pidx[:], op=ALU.is_equal),
        reads=[buf("pidx")], writes=[buf("id")])
    vop(lambda e: e.tensor_copy(out=ident16[:], in_=ident32[:]),
        reads=[buf("id")], writes=[buf("id16")])
    sop(lambda e: e.memzero(ones32[:]), writes=[buf("o32")])
    sop(lambda e: e.add(out=ones32[:], in_=ones32[:], add=1.0),
        reads=[buf("o32")], writes=[buf("o32")])
    vop(lambda e: e.tensor_copy(out=ones16[:], in_=ones32[:]),
        reads=[buf("o32")], writes=[buf("o16")])
    vop(lambda e: e.memset(carry[:], 0.0), writes=[buf("carry")])
    vop(lambda e: e.memset(constt[:, 0:1], LN_EPS), writes=[buf("constt")])
    nc.const_aps.register(F32, LN_EPS, constt[:, 0:1])
    vop(lambda e: e.memset(SEL[:], 0.0), writes=[buf("SEL")])
    for n in range(32):
        dma(SEL[n:n + 1, ts(n)], ones16[0:1, :], reads=[buf("o16")], writes=[buf("SEL")],
            q="gpsimd")

    def colload(dst, src1d, bname, n=128):
        dma(dst, src1d.rearrange("(a b) -> b a", b=n), writes=[buf(bname)], slow=True)

    colload(csb[:, 0:KD], c_row_d, "csb")
    sop(lambda e: e.activation(out=silu_c[:], in_=csb[:, 0:KD], func=AF.Silu),
        reads=[buf("csb")], writes=[buf("siluc")])
    colload(lnw_col[:, 0:KD], ln_w_d, "lnw")
    colload(sgn8[:, 0:KD], ln_b_d, "sgn")
    vop(lambda e: e.tensor_copy(out=lnb16[:], in_=sgn8[:, 0:KD]),
        reads=[buf("sgn")], writes=[buf("lnb16")])
    colload(lng_col[:, 0:KD], ln_g_d, "lng")
    colload(sgn8[:, 0:KD], ln_b4_d, "sgn")
    vop(lambda e: e.tensor_copy(out=lnb4_16[:], in_=sgn8[:, 0:KD]),
        reads=[buf("sgn")], writes=[buf("lnb4_16")])
    colload(bout_col[:, 0:KD], b_out_d, "bout")
    colload(btin_col[:], b_in_d, "btin")
    colload(convb_col[:], conv_b_d, "convb")
    colload(D_col[:], D_vec_d, "Dcol")
    for i_ in range(KI):
        dma(convw_col[:, i_ * 4:(i_ + 1) * 4], conv_w_d[ts(i_), :], writes=[buf("convw")])
    dma(bx_col[:], b_x_d.rearrange("(a b) -> b a", b=32), writes=[buf("bx")], slow=True)

    for ti in range(KI):
        dma(A_sb[:, ts(ti, NS)], A_log_d[ts(ti), :], writes=[buf("A")])
    sop(lambda e: e.activation(out=A_sb[:], in_=A_sb[:], func=AF.Exp),
        reads=[buf("A")], writes=[buf("A")])
    vop(lambda e: e.tensor_scalar(out=A_sb[:], in0=A_sb[:], scalar1=-1.0, scalar2=None,
                                  op0=ALU.mult),
        reads=[buf("A")], writes=[buf("A")], w=256)
    sop(lambda e: e.activation(out=A_sb[:], in_=A_sb[:], func=AF.Exp),
        reads=[buf("A")], writes=[buf("A")])

    for h in range(2):
        dma(srow[:], gamma_d[ts(h, 512)].unsqueeze(0), writes=[buf("srow")])
        top(lambda e: e.matmul(out=psum[5][:, :], lhsT=ones32[0:1, :],
                               rhs=srow[:], start=True, stop=True),
            reads=[buf("srow"), buf("o32")], writes=[pb[5]])
        sop(lambda e, h=h: e.copy(out=gam_rep[:, ts(h, 512)], in_=psum[5][:, :]),
            reads=[pb[5]], writes=[buf("lnA")])

    def rep_target(j):
        return {0: (sa_rep, "sarep", 0.0), 1: (ca1_rep, "carep", 1.0),
                3: (sb_rep, "sbrep", 0.0), 4: (cb1_rep, "cbrep", 1.0)}.get(j)

    for nsl in range(12):
        j, h = nsl // 2, nsl % 2
        dma(prep_stage.rearrange("p (k n) -> p k n", k=KD),
            wmod_d[:, ts(nsl, 512)].rearrange("(k p) n -> p k n", p=128),
            writes=[buf("R4")])
        for k in range(KD):
            top(lambda e, k=k: e.matmul(
                out=psum[6][0:1, :], lhsT=silu_c[:, k:k + 1],
                rhs=prep_stage[:, ts(k, 512)],
                start=(k == 0), stop=(k == KD - 1)),
                reads=[buf("R4"), buf("siluc")], writes=[pb[6]])
        sop(lambda e: e.copy(out=srow[:], in_=psum[6][0:1, :]),
            reads=[pb[6]], writes=[buf("srow")])
        tgt = rep_target(j)
        if tgt is not None:
            dst, dbn, bias = tgt
            top(lambda e: e.matmul(out=psum[5][:, :], lhsT=ones32[0:1, :], rhs=srow[:],
                                   start=True, stop=True),
                reads=[buf("srow"), buf("o32")], writes=[pb[5]])
            if bias == 0.0:
                sop(lambda e, dst=dst, h=h: e.activation(
                    out=dst[:, ts(h, 512)], in_=psum[5][:, :], func=AF.Copy),
                    reads=[pb[5]], writes=[buf(dbn)])
            else:
                sop(lambda e, dst=dst, h=h: e.activation(
                    out=dst[:, ts(h, 512)], in_=psum[5][:, :], func=AF.Copy, bias=1.0),
                    reads=[pb[5], buf("constt")], writes=[buf(dbn)])
        elif j in (2, 5):
            dma(mod_scr[0 if j == 2 else 1, ts(h, 512)].unsqueeze(0), srow[:],
                reads=[buf("srow")], writes=[buf("modscr")], q="gpsimd")
        if "tp_modrow" in taps:
            dma(taps["tp_modrow"][j, ts(h, 512)].unsqueeze(0), srow[:],
                reads=[buf("srow")], writes=[buf("tapm")])

    def clamp(col, cb_):
        vop(lambda e: e.tensor_scalar(out=sgn8[:], in0=col[:], scalar1=0.0, scalar2=2.0,
                                      op0=ALU.is_ge, op1=ALU.mult),
            reads=[cb_], writes=[buf("sgn")])
        vop(lambda e: e.tensor_scalar(out=sgn8[:], in0=sgn8[:], scalar1=-1.0, scalar2=None,
                                      op0=ALU.add),
            reads=[buf("sgn")], writes=[buf("sgn")])
        vop(lambda e: e.tensor_tensor(out=col[:], in0=col[:], in1=sgn8[:], op=ALU.mult),
            reads=[buf("sgn"), cb_], writes=[cb_])
        vop(lambda e: e.tensor_scalar(out=col[:], in0=col[:], scalar1=0.25, scalar2=None,
                                      op0=ALU.max),
            reads=[cb_], writes=[cb_])
        vop(lambda e: e.tensor_tensor(out=col[:], in0=col[:], in1=sgn8[:], op=ALU.mult),
            reads=[buf("sgn"), cb_], writes=[cb_])

    for j in (0, 1):
        for h in range(2):
            dma(srow[:], mod_scr[j, ts(h, 512)].unsqueeze(0),
                reads=[buf("modscr")], writes=[buf("srow")])
            vop(lambda e: e.tensor_scalar(out=btr32[:], in0=srow[:], scalar1=0.0,
                                          scalar2=2.0, op0=ALU.is_ge, op1=ALU.mult),
                reads=[buf("srow")], writes=[buf("btr32")])
            vop(lambda e: e.tensor_scalar(out=btr32[:], in0=btr32[:], scalar1=-1.0,
                                          scalar2=None, op0=ALU.add),
                reads=[buf("sgnr")], writes=[buf("btr32")])
            vop(lambda e: e.tensor_tensor(out=srow[:], in0=srow[:], in1=btr32[:],
                                          op=ALU.mult),
                reads=[buf("btr32"), buf("srow")], writes=[buf("srow")])
            vop(lambda e: e.tensor_scalar(out=srow[:], in0=srow[:], scalar1=0.25,
                                          scalar2=None, op0=ALU.max),
                reads=[buf("srow")], writes=[buf("srow")])
            vop(lambda e: e.tensor_tensor(out=srow[:], in0=srow[:], in1=btr32[:],
                                          op=ALU.mult),
                reads=[buf("btr32"), buf("srow")], writes=[buf("srow")])
            dma(mod_scr[j, ts(h, 512)].unsqueeze(0), srow[:],
                reads=[buf("srow")], writes=[buf("modscr")], q="gpsimd")
            if j == 0:
                top(lambda e: e.matmul(out=psum[5][:, :], lhsT=ones32[0:1, :],
                                       rhs=srow[:], start=True, stop=True),
                    reads=[buf("srow"), buf("o32")], writes=[pb[5]])
                sop(lambda e, h=h: e.copy(out=ga_rep[:, ts(h, 512)], in_=psum[5][:, :]),
                    reads=[pb[5]], writes=[buf("garep")])
    dma(ga_col[:, 0:KD], mod_scr[0].rearrange("(a b) -> b a", b=128),
        reads=[buf("modscr")], writes=[buf("ga")], slow=True)
    dma(gb_col[:, 0:KD], mod_scr[1].rearrange("(a b) -> b a", b=128),
        reads=[buf("modscr")], writes=[buf("gb")], slow=True)
    vop(lambda e: e.tensor_tensor(out=gabo_col[:], in0=ga_col[:], in1=bout_col[:, 0:KD],
                                  op=ALU.mult),
        reads=[buf("ga"), buf("bout")], writes=[buf("gabo")])

    vop(lambda e: e.memset(prep_bf[:, 0:512], 0.0), writes=[buf("R4")])
    top(lambda e: e.matmul(out=psum[7][:, :], lhsT=prep_bf[:, 0:128],
                           rhs=prep_bf[:, 0:512], start=True, stop=False),
        reads=[buf("R4")], writes=[pb[7]])
    for k in range(KD):
        dma(prep_stage[:], W_in_d[ts(k), :], writes=[buf("R4")])
        vop(lambda e: e.tensor_copy(out=prep_bf[:], in_=prep_stage[:]),
            reads=[buf("R4")], writes=[buf("R4")], w=4096)
        for o in range(32):
            top(lambda e, k=k, o=o: e.matmul(
                out=psum[7][:, o:o + 1], lhsT=prep_bf[:, ts(o)],
                rhs=lnb16[:, k:k + 1],
                start=False, stop=(k == KD - 1)),
                reads=[buf("R4"), buf("lnb16")], writes=[pb[7]])
        vop(lambda e, k=k: e.tensor_scalar(out=prep_bf[:], in0=prep_stage[:],
                                           scalar1=lnw_col[:, k:k + 1], scalar2=None,
                                           op0=ALU.mult),
            reads=[buf("R4"), buf("lnw")], writes=[buf("R4")], w=4096)
        for p in range(8):
            dma(Win_scr[k, :, ts(p, 512)], prep_bf[:, ts(p, 512)],
                reads=[buf("R4")], writes=[buf("winscr")], q="gpsimd")
    vop(lambda e: e.tensor_tensor(out=btin_col[:], in0=btin_col[:], in1=psum[7][:, 0:32],
                                  op=ALU.add),
        reads=[pb[7], buf("btin")], writes=[buf("btin")])

    for i in range(KI):
        dma(prep_stage[:, 0:1024], W_out_d[ts(i), :], writes=[buf("R4")])
        vop(lambda e: e.tensor_copy(out=prep_bf[:, 0:1024], in_=prep_stage[:, 0:1024]),
            reads=[buf("R4")], writes=[buf("R4")])
        for g in range(4):
            dma(Wout_scr[g, :, ts(i, 256)], prep_bf[:, g * 256:(g + 1) * 256],
                reads=[buf("R4")], writes=[buf("woscr")], q="gpsimd")

    for i in range(KI):
        dma(prep_stage[:, 0:32], W_x_d[ts(i), :], writes=[buf("R4")])
        vop(lambda e, i=i: e.tensor_copy(out=Wx_sb[:, i * 32:(i + 1) * 32],
                                         in_=prep_stage[:, 0:32]),
            reads=[buf("R4")], writes=[buf("Wx")])

    for r in range(KH):
        dma(prep_stage[:, 0:D], W1_d[ts(r), :], writes=[buf("R4")])
        for cc in range(KD):
            bank = cc % 2
            top(lambda e, cc=cc, bank=bank: e.transpose(
                out=psum[bank][:, 0:128], in_=prep_stage[:, ts(cc)], identity=ident32[:]),
                reads=[buf("R4"), buf("id")], writes=[pb[bank]])
            sop(lambda e, cc=cc, bank=bank: e.copy(
                out=prep_bf[:, cc * 128: cc * 128 + 128], in_=psum[bank][:, 0:128]),
                reads=[pb[bank]], writes=[buf("R4")])
        for cc in range(KD):
            dma(W1_scr[cc, :, slice(r * 128, r * 128 + 128)],
                prep_bf[:, cc * 128: cc * 128 + 128],
                reads=[buf("R4")], writes=[buf("w1scr")], q="gpsimd")

    for r in range(KD):
        dma(prep_stage[:], W2_d[ts(r), :], writes=[buf("R4")])
        for q8 in range(4):
            for jj in range(8):
                cc = q8 * 8 + jj
                bank = jj % 2
                top(lambda e, cc=cc, bank=bank: e.transpose(
                    out=psum[bank][:, 0:128], in_=prep_stage[:, ts(cc)],
                    identity=ident32[:]),
                    reads=[buf("R4"), buf("id")], writes=[pb[bank]])
                sop(lambda e, jj=jj, bank=bank: e.copy(
                    out=prep_bf[:, jj * 128: jj * 128 + 128], in_=psum[bank][:, 0:128]),
                    reads=[pb[bank]], writes=[buf("R4")])
            for jj in range(8):
                cc = q8 * 8 + jj
                dma(W2_scr[cc // 4, :, (cc % 4) * 1024 + r * 128:
                          (cc % 4) * 1024 + r * 128 + 128],
                    prep_bf[:, jj * 128: jj * 128 + 128],
                    reads=[buf("R4")], writes=[buf("w2scr")], q="gpsimd")

    for r in range(KB):
        dma(prep_stage[:, 0:D], router_W_d[ts(r), :], writes=[buf("R4")])
        for cc in range(KD):
            bank = cc % 2
            top(lambda e, cc=cc, bank=bank: e.transpose(
                out=psum[bank][:, 0:128], in_=prep_stage[:, ts(cc)], identity=ident32[:]),
                reads=[buf("R4"), buf("id")], writes=[pb[bank]])
            sop(lambda e, cc=cc, r=r, bank=bank: e.activation(
                out=rW_sb[:, cc * NB + r * 128: cc * NB + r * 128 + 128],
                in_=psum[bank][:, 0:128], func=AF.Identity, scale=lng_col[:, cc:cc + 1]),
                reads=[pb[bank], buf("lng")], writes=[buf("rW")])
    dma(btr32[:], router_b_d.unsqueeze(0), writes=[buf("btr32")])
    for cc in range(KD):
        top(lambda e, cc=cc: e.matmul(out=psum[6][0:1, :],
                                      lhsT=lnb4_16[:, cc:cc + 1],
                                      rhs=rW_sb[:, cc * NB:(cc + 1) * NB],
                                      start=(cc == 0), stop=(cc == KD - 1)),
            reads=[buf("rW"), buf("lnb4_16")], writes=[pb[6]])
    vop(lambda e: e.tensor_tensor(out=btr32[:], in0=btr32[:], in1=psum[6][0:1, :],
                                  op=ALU.add),
        reads=[pb[6], buf("btr32")], writes=[buf("btr32")])
    vop(lambda e: e.tensor_copy(out=btr_row[:], in_=btr32[:]),
        reads=[buf("btr32")], writes=[buf("btr")])

    for (srcd, isU) in ((raw_U_d, True), (raw_V_d, False)):
        for r in range(KB):
            dma(prep_stage[:, 0:D], srcd[ts(r), :], writes=[buf("R4")])
            sop(lambda e: e.activation(out=lnB[:], in_=prep_stage[:, 0:D], func=AF.Square),
                reads=[buf("R4")], writes=[buf("lnB")])
            vop(lambda e: e.reduce_sum(out=uvn[:, 0:1], in_=lnB[:], axis=AX),
                reads=[buf("lnB")], writes=[buf("uvn")])
            sop(lambda e: e.activation(out=uvn[:, 1:2], in_=uvn[:, 0:1], func=AF.Sqrt),
                reads=[buf("uvn")], writes=[buf("uvn")])
            vop(lambda e: e.tensor_scalar(out=uvn[:, 1:2], in0=uvn[:, 1:2], scalar1=FEPS,
                                          scalar2=None, op0=ALU.max),
                reads=[buf("uvn")], writes=[buf("uvn")])
            vop(lambda e: e.reciprocal(out=uvn[:, 2:3], in_=uvn[:, 1:2]),
                reads=[buf("uvn")], writes=[buf("uvn")])
            vop(lambda e: e.tensor_scalar(out=prep_stage[:, 0:D], in0=prep_stage[:, 0:D],
                                          scalar1=uvn[:, 2:3], scalar2=None, op0=ALU.mult),
                reads=[buf("uvn")], writes=[buf("R4")])
            if isU:
                for cc in range(KD):
                    bank = cc % 2
                    top(lambda e, cc=cc, bank=bank: e.transpose(
                        out=psum[bank][:, 0:128], in_=prep_stage[:, ts(cc)],
                        identity=ident32[:]),
                        reads=[buf("R4"), buf("id")], writes=[pb[bank]])
                    sop(lambda e, cc=cc, r=r, bank=bank: e.copy(
                        out=U_sb[:, cc * NB + r * 128: cc * NB + r * 128 + 128],
                        in_=psum[bank][:, 0:128]),
                        reads=[pb[bank]], writes=[buf("U")])
            else:
                vop(lambda e: e.tensor_tensor(
                    out=prep_bf[:, 0:D], in0=prep_stage[:, 0:D],
                    in1=gam_rep[:], op=ALU.mult),
                    reads=[buf("R4"), buf("lnA")], writes=[buf("R4")])
                dma(V_scr[r], prep_bf[:, 0:D], reads=[buf("R4")], writes=[buf("vscr")],
                    q="gpsimd")

    if "tp_gar" in taps:
        dma(taps["tp_gar"][:], ga_rep[:], reads=[buf("garep")], writes=[buf("tapgr")])
    if "tp_prep" in taps:
        for nm, tsr, bn in (("ca", ca1_rep, "carep"), ("sa", sa_rep, "sarep"),
                            ("cb", cb1_rep, "cbrep"), ("sb", sb_rep, "sbrep"),
                            ("gar", ga_rep, "garep")):
            off = {"ca": 0, "sa": 1, "cb": 2, "sb": 3, "gar": 4}[nm] * D
            dma(taps["tp_prep"][:, off:off + D], tsr[:], reads=[buf(bn)],
                writes=[buf("tapp")])
    if "tp_sel" in taps:
        dma(taps["tp_sel"][:], SEL[:], reads=[buf("SEL")], writes=[buf("taps")])
    if "tp_asb" in taps:
        dma(taps["tp_asb"][:], A_sb[:], reads=[buf("A")], writes=[buf("tapa")])
    if "tp_misc" in taps:
        for nm, tsr, bn, off, wdt in (
                ("ga", ga_col, "ga", 0, KD), ("gb", gb_col, "gb", 8, KD),
                ("gabo", gabo_col, "gabo", 16, KD), ("btin", btin_col, "btin", 24, 32),
                ("cw", convw_col, "convw", 56, KI * 4), ("cb2", convb_col, "convb", 120, KI),
                ("dc", D_col, "Dcol", 136, KI)):
            dma(taps["tp_misc"][:, off:off + wdt], tsr[:, 0:wdt], reads=[buf(bn)],
                writes=[buf("tapm2")])
    if "tp_gacol" in taps:
        dma(taps["tp_gacol"][:], ga_col[:], reads=[buf("ga")], writes=[buf("tapgc")])

    inv_d = 1.0 / D

    def ln_stats(src_ap, src_buf):
        sop(lambda e: e.activation(out=lnsq[:], in_=src_ap, func=AF.Square),
            reads=[src_buf], writes=[buf("lnA")])
        vop(lambda e: e.reduce_sum(out=lns[:, 0:1], in_=src_ap, axis=AX),
            reads=[src_buf], writes=[buf("lns")])
        vop(lambda e: e.reduce_sum(out=lns[:, 1:2], in_=lnsq[:], axis=AX),
            reads=[buf("lnA")], writes=[buf("lns")])
        vop(lambda e: e.tensor_scalar(out=lns[:, 2:3], in0=lns[:, 0:1], scalar1=inv_d,
                                      scalar2=None, op0=ALU.mult),
            reads=[buf("lns")], writes=[buf("lns")])
        vop(lambda e: e.tensor_scalar(out=lns[:, 3:4], in0=lns[:, 1:2], scalar1=inv_d,
                                      scalar2=None, op0=ALU.mult),
            reads=[buf("lns")], writes=[buf("lns")])
        vop(lambda e: e.tensor_scalar(out=lns[:, 4:5], in0=lns[:, 2:3],
                                      scalar1=lns[:, 2:3], scalar2=None, op0=ALU.mult),
            reads=[buf("lns")], writes=[buf("lns")])
        vop(lambda e: e.tensor_tensor(out=lns[:, 4:5], in0=lns[:, 3:4], in1=lns[:, 4:5],
                                      op=ALU.subtract),
            reads=[buf("lns")], writes=[buf("lns")])
        sop(lambda e: e.activation(out=lns[:, 4:5], in_=lns[:, 4:5], func=AF.Sqrt,
                                   bias=LN_EPS),
            reads=[buf("lns"), buf("constt")], writes=[buf("lns")])
        vop(lambda e: e.reciprocal(out=lns[:, 5:6], in_=lns[:, 4:5]),
            reads=[buf("lns")], writes=[buf("lns")])
        vop(lambda e: e.tensor_scalar(out=lns[:, 6:7], in0=lns[:, 2:3],
                                      scalar1=lns[:, 5:6], scalar2=-1.0,
                                      op0=ALU.mult, op1=ALU.mult),
            reads=[buf("lns")], writes=[buf("lns")])

    def ln_apply(dst_ap, dst_buf, src_ap, src_buf):
        vop(lambda e: e.tensor_scalar(out=dst_ap, in0=src_ap, scalar1=lns[:, 5:6],
                                      scalar2=lns[:, 6:7], op0=ALU.mult, op1=ALU.add),
            reads=[src_buf, buf("lns")], writes=[dst_buf], w=1024)

    def transpose_to(dst_ap_fn, dst_buf, src_ap, src_buf, ident, ident_buf, nk,
                     banks=(0, 1)):
        for k in range(nk):
            bank = banks[k % len(banks)]
            col = (k // len(banks)) % 4 if len(banks) == 1 else 0
            top(lambda e, k=k, bank=bank, col=col: e.transpose(
                out=psum[bank][:, col * 128: col * 128 + 128], in_=src_ap[:, ts(k)],
                identity=ident[:]),
                reads=[src_buf, ident_buf], writes=[pb[bank]])
            sop(lambda e, k=k, bank=bank, col=col: e.copy(
                out=dst_ap_fn(k), in_=psum[bank][:, col * 128: col * 128 + 128]),
                reads=[pb[bank]], writes=[dst_buf])

    out_bufs = []

    for c in range(NCH):
        t0 = c * CH
        for tt in range(NTT):
            dma(x_tok[:, ts(tt, D)], X_in[slice(t0 + tt * 128, t0 + (tt + 1) * 128), :],
                writes=[buf("x_tok")])

        for tt in range(NTT):
            xs = x_tok[:, ts(tt, D)]
            ln_stats(xs, buf("x_tok"))
            ln_apply(lnB[:], buf("lnB"), xs, buf("x_tok"))
            vop(lambda e: e.tensor_tensor(out=lnB[:], in0=lnB[:], in1=ca1_rep[:],
                                          op=ALU.mult),
                reads=[buf("carep")], writes=[buf("lnB")], w=1024)
            vop(lambda e: e.tensor_tensor(out=lnB[:], in0=lnB[:], in1=sa_rep[:],
                                          op=ALU.add),
                reads=[buf("sarep")], writes=[buf("lnB")], w=1024)
            if "tp_h1" in taps:
                dma(taps["tp_h1"][slice(t0 + tt * 128, t0 + tt * 128 + 128), :], lnB[:],
                    reads=[buf("lnB")], writes=[buf("tapg")])
            vop(lambda e: e.tensor_tensor(out=lnA[:], in0=lnB[:], in1=ga_rep[:],
                                          op=ALU.mult),
                reads=[buf("lnB"), buf("garep")], writes=[buf("lnA")], w=1024)
            vop(lambda e, tt=tt: e.tensor_tensor(out=x_tok[:, ts(tt, D)],
                                                 in0=x_tok[:, ts(tt, D)], in1=lnA[:],
                                                 op=ALU.add),
                reads=[buf("lnA")], writes=[buf("x_tok")], w=1024)
            if "tp_xup" in taps:
                dma(taps["tp_xup"][slice(t0 + tt * 128, t0 + tt * 128 + 128), :],
                    x_tok[:, ts(tt, D)], reads=[buf("x_tok")], writes=[buf("tapxu")])
            ln_stats(lnB[:], buf("lnB"))
            ln_apply(lnA[:], buf("lnA"), lnB[:], buf("lnB"))
            transpose_to(lambda k, tt=tt: xn2f[:, k * CH + tt * 128: k * CH + tt * 128 + 128],
                         buf("R5"), lnA, buf("lnA"), ident32, buf("id"), KD)
            if "tp_xn1" in taps:
                dma(taps["tp_xn1"][slice(t0 + tt * 128, t0 + tt * 128 + 128), :], lnA[:],
                    reads=[buf("lnA")], writes=[buf("tapf")])

        if "tp_xn2f" in taps:
            for k in range(KD):
                dma(taps["tp_xn2f"][ts(k), t0:t0 + CH], xn2f[:, ts(k, CH)],
                    reads=[buf("R5")], writes=[buf("tap5b")])
        for i in range(KI):
            if c == 0:
                vop(lambda e, i=i: e.memset(x_main[:, i * 515: i * 515 + 3], 0.0),
                    writes=[buf("R1")])
            else:
                vop(lambda e, i=i: e.tensor_copy(out=x_main[:, i * 515: i * 515 + 3],
                                                 in_=ctail[:, i * 3:(i + 1) * 3]),
                    reads=[buf("ctail")], writes=[buf("R1")])
        for og in range(8):
            for k in range(KD):
                dma(wbufA[:, ts(k, 512)], Win_scr[k, :, ts(og, 512)], reads=[buf("winscr")], writes=[buf("wA")])
            for oi in range(4):
                o = og * 4 + oi
                bank = 2 + (o % 2)
                for k in range(KD):
                    top(lambda e, k=k, oi=oi, bank=bank: e.matmul(
                        out=psum[bank][:, :],
                        lhsT=wbufA[:, k * 512 + oi * 128: k * 512 + oi * 128 + 128],
                        rhs=xn2f[:, ts(k, CH)], start=(k == 0), stop=(k == KD - 1)),
                        reads=[buf("wA"), buf("R5")], writes=[pb[bank]])
                if o < 16:
                    sop(lambda e, o=o, bank=bank: e.activation(
                        out=x_main[:, o * 515 + 3: o * 515 + 3 + CH],
                        in_=psum[bank][:, :], func=AF.Identity, bias=btin_col[:, o:o + 1]),
                        reads=[pb[bank], buf("btin")], writes=[buf("R1")])
                else:
                    sop(lambda e, o=o, bank=bank: e.activation(
                        out=gate_s[:, (o - 16) * CH: (o - 15) * CH],
                        in_=psum[bank][:, :], func=AF.Silu, bias=btin_col[:, o:o + 1]),
                        reads=[pb[bank], buf("btin")], writes=[buf("R3")])

        for i in range(KI):
            base = i * 515
            vop(lambda e, base=base, i=i: e.tensor_scalar(
                out=convacc[:], in0=x_main[:, base: base + CH],
                scalar1=convw_col[:, i * 4: i * 4 + 1], scalar2=None, op0=ALU.mult),
                reads=[buf("R1"), buf("convw")], writes=[buf("cacc")], w=CH)
            for kk in range(1, 4):
                vop(lambda e, base=base, i=i, kk=kk: e.scalar_tensor_tensor(
                    out=convacc[:], in0=x_main[:, base + kk: base + kk + CH],
                    scalar=convw_col[:, i * 4 + kk: i * 4 + kk + 1], in1=convacc[:],
                    op0=ALU.mult, op1=ALU.add),
                    reads=[buf("R1"), buf("convw"), buf("cacc")], writes=[buf("cacc")], w=CH)
            vop(lambda e, i=i: e.tensor_copy(out=ctail[:, i * 3:(i + 1) * 3],
                                             in_=x_main[:, i * 515 + CH: i * 515 + CH + 3]),
                reads=[buf("R1")], writes=[buf("ctail")])
            sop(lambda e, i=i: e.activation(out=x_conv[:, ts(i, CH)], in_=convacc[:],
                                            func=AF.Silu, bias=convb_col[:, i:i + 1]),
                reads=[buf("cacc"), buf("convb")], writes=[buf("R2")])

        if "tp_xmain" in taps:
            for i in range(KI):
                dma(taps["tp_xmain"][ts(i), t0:t0 + CH],
                    x_main[:, i * 515 + 3: i * 515 + 3 + CH],
                    reads=[buf("R1")], writes=[buf("tap8")])
        if "tp_gate" in taps:
            for i in range(KI):
                dma(taps["tp_gate"][ts(i), t0:t0 + CH], gate_s[:, ts(i, CH)],
                    reads=[buf("R3")], writes=[buf("tap9")])
        for i in range(KI):
            top(lambda e, i=i: e.matmul(out=psum[4][0:32, :],
                                        lhsT=Wx_sb[:, i * 32:(i + 1) * 32],
                                        rhs=x_conv[:, ts(i, CH)],
                                        start=(i == 0), stop=(i == KI - 1)),
                reads=[buf("Wx"), buf("R2")], writes=[pb[4]])
        sop(lambda e: e.activation(out=ssm_bf[:], in_=psum[4][0:32, :], func=AF.Identity,
                                   bias=bx_col[:]),
            reads=[pb[4], buf("bx")], writes=[buf("ssm")])
        if "tp_ssm" in taps:
            dma(taps["tp_ssm"][:, t0:t0 + CH], ssm_bf[:], reads=[buf("ssm")],
                writes=[buf("tap1")])

        for n in range(2 * NS):
            bank = 5 + (n % 2)
            top(lambda e, n=n, bank=bank: e.matmul(
                out=psum[bank][:, :], lhsT=SEL[:, ts(n)], rhs=ssm_bf[:, :],
                start=True, stop=True),
                reads=[buf("ssm"), buf("SEL")], writes=[pb[bank]])
            dstt = B_rep if n < NS else C_rep
            nn = n % NS
            sop(lambda e, nn=nn, bank=bank, dstt=dstt: e.copy(
                out=dstt[:, ts(nn, CH)], in_=psum[bank][:, :]),
                reads=[pb[bank]], writes=[buf("R4")])

        if c == 0 and "tp_brep" in taps:
            dma(taps["tp_brep"][:], B_rep[:], reads=[buf("R4")], writes=[buf("tapbr")])
        if c == 0 and "tp_crep" in taps:
            dma(taps["tp_crep"][:], C_rep[:], reads=[buf("R4")], writes=[buf("tapcr")])
        for i in range(KI):
            for n in range(NS):
                vop(lambda e, i=i, n=n: e.tensor_tensor(
                    out=Zbuf[:], in0=x_conv[:, ts(i, CH)], in1=B_rep[:, ts(n, CH)],
                    op=ALU.mult),
                    reads=[buf("R2"), buf("R4")], writes=[buf("Z")], w=CH)
                vop(lambda e, i=i, n=n: e.tensor_tensor_scan(
                    out=H_slab[:, ts(n, CH)],
                    data0=A_sb[:, i * NS + n: i * NS + n + 1].to_broadcast([128, CH]),
                    data1=Zbuf[:], initial=carry[:, i * NS + n: i * NS + n + 1],
                    op0=ALU.mult, op1=ALU.add),
                    reads=[buf("Z"), buf("A"), buf("carry")], writes=[buf("R5")], w=CH)
            if c == 0 and i == 0 and "tp_H0" in taps:
                dma(taps["tp_H0"][:], H_slab[:], reads=[buf("R5")], writes=[buf("tapH")])
            vop(lambda e, i=i: e.tensor_copy(out=carry[:, ts(i, NS)],
                                             in_=H_slab[:, CH - 1::CH]),
                reads=[buf("R5")], writes=[buf("carry")])
            vop(lambda e: e.tensor_tensor(out=W_slab[:], in0=H_slab[:], in1=C_rep[:],
                                          op=ALU.mult),
                reads=[buf("R5"), buf("R4")], writes=[buf("Wsl")], w=NS * CH)
            vop(lambda e: e.tensor_tensor(out=W_slab[:, 0:8 * CH], in0=W_slab[:, 0:8 * CH],
                                          in1=W_slab[:, 8 * CH:], op=ALU.add),
                writes=[buf("Wsl")], w=8 * CH)
            vop(lambda e: e.tensor_tensor(out=W_slab[:, 0:4 * CH], in0=W_slab[:, 0:4 * CH],
                                          in1=W_slab[:, 4 * CH:8 * CH], op=ALU.add),
                writes=[buf("Wsl")], w=4 * CH)
            vop(lambda e: e.tensor_tensor(out=W_slab[:, 0:2 * CH], in0=W_slab[:, 0:2 * CH],
                                          in1=W_slab[:, 2 * CH:4 * CH], op=ALU.add),
                writes=[buf("Wsl")], w=2 * CH)
            vop(lambda e: e.tensor_tensor(out=yb[:], in0=W_slab[:, 0:CH],
                                          in1=W_slab[:, CH:2 * CH], op=ALU.add),
                reads=[buf("Wsl")], writes=[buf("yb")], w=CH)
            vop(lambda e, i=i: e.scalar_tensor_tensor(
                out=convacc[:], in0=x_conv[:, ts(i, CH)], scalar=D_col[:, i:i + 1],
                in1=yb[:], op0=ALU.mult, op1=ALU.add),
                reads=[buf("R2"), buf("Dcol"), buf("yb")], writes=[buf("cacc")], w=CH)
            vop(lambda e, i=i: e.tensor_tensor(out=ytot[:, ts(i, CH)], in0=convacc[:],
                                               in1=gate_s[:, ts(i, CH)], op=ALU.mult),
                reads=[buf("cacc"), buf("R3")], writes=[buf("R1")], w=CH)
        if "tp_xconv" in taps:
            for i in range(KI):
                dma(taps["tp_xconv"][ts(i), t0:t0 + CH], x_conv[:, ts(i, CH)],
                    reads=[buf("R2")], writes=[buf("tap2")])
        if "tp_ytot" in taps:
            for i in range(KI):
                dma(taps["tp_ytot"][ts(i), t0:t0 + CH], ytot[:, ts(i, CH)],
                    reads=[buf("R1")], writes=[buf("tap3")])

        for g in range(4):
            dma(wbufA[:, 0:KI * 256], Wout_scr[g], reads=[buf("woscr")], writes=[buf("wA")])
            for dh in range(2):
                dt = g * 2 + dh
                bank = 2 + dh
                for i in range(KI):
                    top(lambda e, i=i, dh=dh, bank=bank: e.matmul(
                        out=psum[bank][:, :],
                        lhsT=wbufA[:, i * 256 + dh * 128: i * 256 + dh * 128 + 128],
                        rhs=ytot[:, ts(i, CH)], start=(i == 0), stop=(i == KI - 1)),
                        reads=[buf("wA"), buf("R1")], writes=[pb[bank]])
                sop(lambda e, dt=dt, bank=bank: e.activation(
                    out=t1f[:], in_=psum[bank][:, :], func=AF.Identity,
                    scale=ga_col[:, dt:dt + 1], bias=gabo_col[:, dt:dt + 1]),
                    reads=[pb[bank], buf("ga"), buf("gabo")], writes=[buf("yb")])
                transpose_to(lambda k, dt=dt: X1tok[:, k * D + dt * 128:
                                                    k * D + dt * 128 + 128],
                             buf("R2"), t1f, buf("yb"), ident32, buf("id"), NTT)
        for tt in range(NTT):
            vop(lambda e, tt=tt: e.tensor_tensor(
                out=X1tok[:, ts(tt, D)], in0=X1tok[:, ts(tt, D)], in1=x_tok[:, ts(tt, D)],
                op=ALU.add),
                reads=[buf("x_tok")], writes=[buf("R2")], w=1024)
        if "tp_X1" in taps:
            for tt in range(NTT):
                dma(taps["tp_X1"][slice(t0 + tt * 128, t0 + tt * 128 + 128), :],
                    X1tok[:, ts(tt, D)], reads=[buf("R2")], writes=[buf("tap4")])

        for tt in range(NTT):
            xs = X1tok[:, ts(tt, D)]
            ln_stats(xs, buf("R2"))
            ln_apply(lnB[:], buf("lnB"), xs, buf("R2"))
            vop(lambda e: e.tensor_tensor(out=lnB[:], in0=lnB[:], in1=cb1_rep[:],
                                          op=ALU.mult),
                reads=[buf("cbrep")], writes=[buf("lnB")], w=1024)
            vop(lambda e: e.tensor_tensor(out=lnB[:], in0=lnB[:], in1=sb_rep[:],
                                          op=ALU.add),
                reads=[buf("sbrep")], writes=[buf("lnB")], w=1024)
            transpose_to(lambda k, tt=tt: hff[:, k * CH + tt * 128: k * CH + tt * 128 + 128],
                         buf("R3"), lnB, buf("lnB"), ident32, buf("id"), KD)
            ln_stats(lnB[:], buf("lnB"))
            ln_apply(lnA[:], buf("lnA"), lnB[:], buf("lnB"))
            transpose_to(lambda k, tt=tt: xn4f[:, k * CH + tt * 128: k * CH + tt * 128 + 128],
                         buf("R3"), lnA, buf("lnA"), ident32, buf("id"), KD)
        if "tp_hff" in taps:
            for k in range(KD):
                dma(taps["tp_hff"][ts(k), t0:t0 + CH], hff[:, ts(k, CH)],
                    reads=[buf("R3")], writes=[buf("tap5")])
        for tt in range(NTT):
            for k in range(KD):
                top(lambda e, k=k, tt=tt: e.matmul(
                    out=psum[4][:, :],
                    lhsT=xn4f[:, k * CH + tt * 128: k * CH + tt * 128 + 128],
                    rhs=rW_sb[:, k * NB:(k + 1) * NB], start=(k == 0), stop=False),
                    reads=[buf("R3"), buf("rW")], writes=[pb[4]])
            top(lambda e: e.matmul(out=psum[4][:, :], lhsT=ones16[0:1, :], rhs=btr_row[:],
                                   start=False, stop=True),
                reads=[buf("btr"), buf("o16")], writes=[pb[4]])
            sop(lambda e: e.copy(out=r_tok[:], in_=psum[4][:, :]),
                reads=[pb[4]], writes=[buf("R5")])
            for k in range(KD):
                top(lambda e, k=k, tt=tt: e.matmul(
                    out=psum[5][:, :],
                    lhsT=hff[:, k * CH + tt * 128: k * CH + tt * 128 + 128],
                    rhs=U_sb[:, k * NB:(k + 1) * NB], start=(k == 0), stop=(k == KD - 1)),
                    reads=[buf("R3"), buf("U")], writes=[pb[5]])
            sop(lambda e: e.copy(out=hlat[:], in_=psum[5][:, :]),
                reads=[pb[5]], writes=[buf("R5")])
            vop(lambda e: e.max(out=tks[:, 0:8], in_=r_tok[:]),
                reads=[buf("R5")], writes=[buf("tks")])
            vop(lambda e: e.tensor_scalar(out=alphaf[:], in0=r_tok[:], scalar1=TAU,
                                          scalar2=-TAU, op0=ALU.min, op1=ALU.max),
                writes=[buf("R5")], w=512)
            sop(lambda e: e.activation(out=alphaf[:], in_=alphaf[:], func=AF.Exp),
                reads=[buf("R5")], writes=[buf("R5")], w=512)
            sop(lambda e: e.activation(out=alphaf[:], in_=alphaf[:], func=AF.Ln, bias=1.0),
                reads=[buf("R5"), buf("constt")], writes=[buf("R5")], w=512)
            vop(lambda e: e.match_replace(out=zap[:], in_to_replace=tks[:, 0:8],
                                          in_values=r_tok[:], imm_value=-1e30),
                reads=[buf("tks")], writes=[buf("R5")])
            vop(lambda e: e.tensor_scalar(out=tks[:, 8:16], in0=tks[:, 0:8], scalar1=TAU,
                                          scalar2=-TAU, op0=ALU.min, op1=ALU.max),
                reads=[buf("tks")], writes=[buf("tks")])
            sop(lambda e: e.activation(out=tks[:, 8:16], in_=tks[:, 8:16], func=AF.Exp),
                reads=[buf("tks")], writes=[buf("tks")])
            sop(lambda e: e.activation(out=tks[:, 8:16], in_=tks[:, 8:16], func=AF.Ln,
                                       bias=1.0),
                reads=[buf("tks"), buf("constt")], writes=[buf("tks")])
            vop(lambda e: e.reduce_sum(out=tks[:, 16:17], in_=tks[:, 8:16], axis=AX),
                reads=[buf("tks")], writes=[buf("tks")])
            sop(lambda e: e.activation(out=tks[:, 17:18], in_=tks[:, 16:17], func=AF.Tanh),
                reads=[buf("tks")], writes=[buf("tks")])
            vop(lambda e: e.tensor_scalar(out=tks[:, 18:19], in0=tks[:, 16:17],
                                          scalar1=FEPS, scalar2=None, op0=ALU.add),
                reads=[buf("tks")], writes=[buf("tks")])
            vop(lambda e: e.reciprocal(out=tks[:, 19:20], in_=tks[:, 18:19]),
                reads=[buf("tks")], writes=[buf("tks")])
            vop(lambda e: e.tensor_scalar(out=tks[:, 20:21], in0=tks[:, 17:18],
                                          scalar1=tks[:, 19:20], scalar2=None,
                                          op0=ALU.mult),
                reads=[buf("tks")], writes=[buf("tks")])
            vop(lambda e: e.tensor_tensor(out=zap[:], in0=r_tok[:], in1=zap[:],
                                          op=ALU.not_equal),
                writes=[buf("R5")], w=512)
            vop(lambda e: e.tensor_tensor(out=zap[:], in0=zap[:], in1=alphaf[:],
                                          op=ALU.mult),
                writes=[buf("R5")], w=512)
            vop(lambda e: e.tensor_tensor(out=zap[:], in0=zap[:], in1=hlat[:],
                                          op=ALU.mult),
                writes=[buf("R5")], w=512)
            vop(lambda e: e.tensor_scalar(out=G_tok[:], in0=zap[:],
                                          scalar1=tks[:, 20:21], scalar2=None,
                                          op0=ALU.mult),
                reads=[buf("tks")], writes=[buf("R5")], w=512)
            if "tp_r" in taps:
                dma(taps["tp_r"][slice(t0 + tt * 128, t0 + tt * 128 + 128), :], r_tok[:],
                    reads=[buf("R5")], writes=[buf("tap6")])
            if "tp_G" in taps:
                dma(taps["tp_G"][slice(t0 + tt * 128, t0 + tt * 128 + 128), :], G_tok[:],
                    reads=[buf("R5")], writes=[buf("tap7")])
            transpose_to(lambda k, tt=tt: Gf[:, k * CH + tt * 128: k * CH + tt * 128 + 128],
                         buf("R5"), G_tok, buf("R5"), ident32, buf("id"), KB,
                         banks=(6, 7))

        for og in range(8):
            for k in range(KD):
                dma(wbufA[:, ts(k, 512)], W1_scr[k, :, ts(og, 512)], reads=[buf("w1scr")], writes=[buf("wA")])
            for oi in range(4):
                o = og * 4 + oi
                bank = 2 + (o % 2)
                for k in range(KD):
                    top(lambda e, k=k, oi=oi, bank=bank: e.matmul(
                        out=psum[bank][:, :],
                        lhsT=wbufA[:, k * 512 + oi * 128: k * 512 + oi * 128 + 128],
                        rhs=hff[:, ts(k, CH)], start=(k == 0), stop=(k == KD - 1)),
                        reads=[buf("wA"), buf("R3")], writes=[pb[bank]])
                sop(lambda e, o=o, bank=bank: e.activation(
                    out=g1[:, ts(o, CH)], in_=psum[bank][:, :], func=AF.Gelu),
                    reads=[pb[bank]], writes=[buf("R4")])

        for kb in range(KB):
            dma(wbufA[:, kb * D:(kb + 1) * D], V_scr[kb], reads=[buf("vscr")], writes=[buf("wA")])
        for dt in range(8):
            for kb in range(KB):
                top(lambda e, dt=dt, kb=kb: e.matmul(
                    out=psum[dt][:, :],
                    lhsT=wbufA[:, kb * D + dt * 128: kb * D + dt * 128 + 128],
                    rhs=Gf[:, ts(kb, CH)], start=(kb == 0), stop=False),
                    reads=[buf("wA"), buf("R5")], writes=[pb[dt]])
        for q8 in range(8):
            dma(wbufA[:, 0:4096], W2_scr[q8], reads=[buf("w2scr")], writes=[buf("wA")])
            for j in range(4):
                kh = q8 * 4 + j
                for dt in range(8):
                    top(lambda e, j=j, dt=dt, kh=kh: e.matmul(
                        out=psum[dt][:, :],
                        lhsT=wbufA[:, j * 1024 + dt * 128: j * 1024 + dt * 128 + 128],
                        rhs=g1[:, ts(kh, CH)], start=False, stop=(kh == KH - 1)),
                        reads=[buf("wA"), buf("R4")], writes=[pb[dt]])
        for dt in range(8):
            sop(lambda e, dt=dt: e.activation(out=t1f[:], in_=psum[dt][:, :], func=AF.Identity,
                                              scale=gb_col[:, dt:dt + 1]),
                reads=[pb[dt], buf("gb")], writes=[buf("yb")])
            transpose_to(lambda k, dt=dt: x2tok[:, k * D + dt * 128: k * D + dt * 128 + 128],
                         buf("R1"), t1f, buf("yb"), ident32, buf("id"), NTT,
                         banks=(dt,))
        for tt in range(NTT):
            vop(lambda e, tt=tt: e.tensor_tensor(
                out=x2tok[:, ts(tt, D)], in0=x2tok[:, ts(tt, D)], in1=X1tok[:, ts(tt, D)],
                op=ALU.add),
                reads=[buf("R2")], writes=[buf("R1")], w=1024)
            ob = buf(f"out{c}_{tt}")
            dma(OUT[slice(t0 + tt * 128, t0 + tt * 128 + 128), :], x2tok[:, ts(tt, D)],
                reads=[buf("R1")], writes=[ob], q="gpsimd")
            out_bufs.append(ob)

    P.op("gpsimd", lambda e: e.nop(), reads=out_bufs)

    with nc.Block() as block:
        P.emit(block)
    es.close()
    return nc


_NC_CACHE = {}


def get_program(debug_taps=()):
    key = tuple(sorted(debug_taps))
    if key not in _NC_CACHE:
        _NC_CACHE[key] = build_program(debug_taps=key)
    return _NC_CACHE[key]


def make_in_map(inputs, b):
    return {
        "x": np.ascontiguousarray(inputs["L0_attn_x"][b]),
        "ln_w": inputs["L0_attn_ln_w"],
        "ln_b": inputs["L0_attn_ln_b"],
        "W_in": inputs["L0_attn_W_in"],
        "b_in": inputs["L0_attn_b_in"],
        "conv_w": np.ascontiguousarray(inputs["L0_attn_conv_w"].reshape(DI, 4)),
        "conv_b": inputs["L0_attn_conv_b"],
        "A_log": inputs["L0_attn_A_log"],
        "Dvec": inputs["L0_attn_D"],
        "W_x": inputs["L0_attn_W_x"],
        "b_x": inputs["L0_attn_b_x"],
        "W_out": inputs["L0_attn_W_out"],
        "b_out": inputs["L0_attn_b_out"],
        "W1": inputs["L0_ffn_W1"],
        "W2": inputs["L0_ffn_W2"],
        "ln_g": inputs["L0_ffn_ln_g"],
        "ln_b4": inputs["L0_ffn_ln_b"],
        "router_W": inputs["L0_ffn_router_W"],
        "router_b": inputs["L0_ffn_router_b"],
        "raw_U": inputs["L0_ffn_raw_U"],
        "raw_V": inputs["L0_ffn_raw_V"],
        "gamma": np.ascontiguousarray(inputs["L0_ffn_gamma"].reshape(D)),
        "c_row": np.ascontiguousarray(inputs["L0_c"][b]),
        "wmod": inputs["L0_wmod"],
    }


def kernel(**inputs):
    inputs = {k: np.ascontiguousarray(np.asarray(v, dtype=np.float32))
              for k, v in inputs.items()}
    nc = get_program()
    in_maps = [make_in_map(inputs, b) for b in range(8)]
    res = run_bass_kernel_spmd(nc, in_maps, core_ids=list(range(8)))
    out = np.stack([np.asarray(res.results[b]["out"]) for b in range(8)], axis=0)
    return out


# version 03
# speedup vs baseline: 1.2435x; 1.2435x over previous
import numpy as np
from contextlib import ExitStack

import concourse.bass as bass
import concourse.mybir as mybir
from concourse.bass_utils import run_bass_kernel_spmd

F32 = mybir.dt.float32
BF16 = mybir.dt.bfloat16
AF = mybir.ActivationFunctionType
ALU = mybir.AluOpType
AX = mybir.AxisListType.X

D = 1024
DI = 2048
T = 4096
NS = 16
NB = 512
HF = 4096
CH = 512
NCH = T // CH
NTT = CH // 128
KD = D // 128
KI = DI // 128
KH = HF // 128
KB = NB // 128
LN_EPS = 1e-5
TAU = 10.0
FEPS = 1e-6

ENGS = ("sync", "gpsimd", "vector", "scalar", "tensor")
GAP_WIDE = 1
GAP_NARROW = 4


def ts(i, n=128):
    return slice(i * n, (i + 1) * n)


class Buf:
    __slots__ = ("writer", "readers", "name")

    def __init__(self, name=""):
        self.writer = None
        self.readers = []
        self.name = name


class Prog:
    def __init__(self, nc, es):
        self.nc = nc
        self.ops = {e: [] for e in ENGS}
        self.sems = {e: es.enter_context(nc.semaphore(f"sem_{e}")) for e in ENGS}
        self.count = {e: 0 for e in ENGS}
        self.ord = {e: 0 for e in ENGS}
        self.waited = {e: set() for e in ENGS}

    def op(self, eng, fn, reads=(), writes=(), dma=False, w=0):
        rdeps = [b.writer for b in reads if b.writer is not None]
        wdeps = []
        for b in writes:
            if b.writer is not None:
                wdeps.append(b.writer)
            wdeps.extend(b.readers)
        inc = 16 if dma else 1
        pre = dict(self.count)
        self.count[eng] += inc
        self.ord[eng] += 1
        tok = (eng, self.count[eng], dma, self.ord[eng], w)
        wl = {}
        for (e, v, pdma, pord, pw) in rdeps:
            if pdma:
                if v > wl.get(e, 0):
                    wl[e] = v
            elif e != eng:
                if v > wl.get(e, 0):
                    wl[e] = v
            else:
                dist = self.ord[eng] - pord
                if dist < GAP_WIDE or (dist < GAP_NARROW and pw < 256):
                    if v > wl.get(e, 0):
                        wl[e] = v
        for (e, v, pdma, pord, pw) in wdeps:
            if pdma:
                if v > wl.get(e, 0):
                    wl[e] = v
            elif e != eng:
                if v > wl.get(e, 0):
                    wl[e] = v
            else:
                dist = self.ord[eng] - pord
                if dist < GAP_WIDE or (dist < GAP_NARROW and pw < 256):
                    if v > wl.get(e, 0):
                        wl[e] = v
        self.ops[eng].append((sorted(wl.items()), fn, dma))
        for (e, v) in wl.items():
            self.waited[e].add(v)
        for b in reads:
            b.readers.append(tok)
        for b in writes:
            b.writer = tok
            b.readers = []
        return tok

    def emit(self, block):
        sems = self.sems
        waited = self.waited

        def make_body(eng_name, eng_ops):
            def body(eng):
                hwm = {x: 0 for x in ENGS}
                pend = 0
                val = 0
                for waits, fn, isdma in eng_ops:
                    for (se, sv) in waits:
                        if sv > hwm[se]:
                            eng.wait_ge(sems[se], sv)
                            hwm[se] = sv
                    ins = fn(eng)
                    if isdma:
                        val += 16
                        ins.then_inc(sems[eng_name], 16)
                    else:
                        val += 1
                        pend += 1
                        if val in waited[eng_name]:
                            eng.drain(fusable=True).then_inc(sems[eng_name], pend)
                            pend = 0
            return body

        for e in ENGS:
            getattr(block, e)(make_body(e, self.ops[e]))


def build_program(debug_taps=()):
    nc = bass.Bass("TRN2", target_bir_lowering=False)
    es = ExitStack()

    def din(name, shape):
        return nc.dram_tensor(name, list(shape), F32, kind="ExternalInput").ap()

    X_in = din("x", (T, D))
    ln_w_d = din("ln_w", (D,))
    ln_b_d = din("ln_b", (D,))
    W_in_d = din("W_in", (D, 2 * DI))
    b_in_d = din("b_in", (2 * DI,))
    conv_w_d = din("conv_w", (DI, 4))
    conv_b_d = din("conv_b", (DI,))
    A_log_d = din("A_log", (DI, NS))
    D_vec_d = din("Dvec", (DI,))
    W_x_d = din("W_x", (DI, 2 * NS))
    b_x_d = din("b_x", (2 * NS,))
    W_out_d = din("W_out", (DI, D))
    b_out_d = din("b_out", (D,))
    W1_d = din("W1", (HF, D))
    W2_d = din("W2", (D, HF))
    ln_g_d = din("ln_g", (D,))
    ln_b4_d = din("ln_b4", (D,))
    router_W_d = din("router_W", (NB, D))
    router_b_d = din("router_b", (NB,))
    raw_U_d = din("raw_U", (NB, D))
    raw_V_d = din("raw_V", (NB, D))
    gamma_d = din("gamma", (D,))
    c_row_d = din("c_row", (D,))
    wmod_d = din("wmod", (D, 6 * D))

    OUT = nc.dram_tensor("out", [T, D], F32, kind="ExternalOutput").ap()

    taps = {}
    TAPS = {
        "tp_xconv": ([DI, T], BF16),
        "tp_ssm": ([32, T], BF16),
        "tp_ytot": ([DI, T], BF16),
        "tp_X1": ([T, D], F32),
        "tp_r": ([T, NB], F32),
        "tp_G": ([T, NB], F32),
        "tp_hff": ([D, T], BF16),
        "tp_xn2f": ([D, T], BF16),
        "tp_xmain": ([DI, T], BF16),
        "tp_gate": ([DI, T], BF16),
        "tp_modrow": ([6, D], F32),
        "tp_xn1": ([T, D], F32),
        "tp_h1": ([T, D], F32),
        "tp_xup": ([T, D], F32),
        "tp_H0": ([128, NS * CH], BF16),
        "tp_gar": ([128, D], BF16),
        "tp_gacol": ([128, KD], F32),
        "tp_prep": ([128, 6 * D], BF16),
        "tp_sel": ([32, 32 * 128], BF16),
        "tp_asb": ([128, KI * NS], F32),
        "tp_misc": ([128, 160], F32),
        "tp_brep": ([128, NS * CH], BF16),
        "tp_crep": ([128, NS * CH], BF16),
    }
    for name in debug_taps:
        shp, dt_ = TAPS[name]
        taps[name] = nc.dram_tensor(name, list(shp), dt_, kind="ExternalOutput").ap()

    Win_scr = nc.dram_tensor("Win_scr", [8, 128, KD * 512], BF16, kind="Internal").ap()
    Wout_scr = nc.dram_tensor("Wout_scr", [4, 128, KI * 256], BF16, kind="Internal").ap()
    W1_scr = nc.dram_tensor("W1_scr", [8, 128, KD * 512], BF16, kind="Internal").ap()
    W2_scr = nc.dram_tensor("W2_scr", [8, 128, 4 * 1024], BF16, kind="Internal").ap()
    V_scr = nc.dram_tensor("V_scr", [KB, 128, D], BF16, kind="Internal").ap()
    mod_scr = nc.dram_tensor("mod_scr", [2, D], F32, kind="Internal").ap()

    def asb(name, shape, dtype=F32):
        return es.enter_context(nc.sbuf_tensor(name, list(shape), dtype))

    ident32 = asb("ident32", [128, 128], F32)
    ident16 = asb("ident16", [128, 128], BF16)
    pidx = asb("pidx", [128, 128], F32)
    ones16 = asb("ones16", [1, 128], BF16)
    ones32 = asb("ones32", [1, 128], F32)
    SEL = asb("SEL", [32, 32 * 128], BF16)
    ca1_rep = asb("ca1_rep", [128, D], BF16)
    ga_rep = asb("ga_rep", [128, D], BF16)
    sa_rep = asb("sa_rep", [128, D], BF16)
    cb1_rep = asb("cb1_rep", [128, D], BF16)
    sb_rep = asb("sb_rep", [128, D], BF16)
    ga_col = asb("ga_col", [128, KD], F32)
    gb_col = asb("gb_col", [128, KD], F32)
    gabo_col = asb("gabo_col", [128, KD], F32)
    sgn8 = asb("sgn8", [128, KD], F32)
    lnw_col = asb("lnw_col", [128, KD], F32)
    lnb16 = asb("lnb16", [128, KD], BF16)
    lng_col = asb("lng_col", [128, KD], F32)
    lnb4_16 = asb("lnb4_16", [128, KD], BF16)
    bout_col = asb("bout_col", [128, KD], F32)
    btin_col = asb("btin_col", [128, 32], F32)
    bx_col = asb("bx_col", [32, 1], F32)
    btr_row = asb("btr_row", [1, NB], BF16)
    btr32 = asb("btr32", [1, NB], F32)
    A_sb = asb("A_sb", [128, KI * NS], F32)
    D_col = asb("D_col", [128, KI], F32)
    convw_col = asb("convw_col", [128, KI * 4], F32)
    convb_col = asb("convb_col", [128, KI], F32)
    carry = asb("carry", [128, KI * NS], BF16)
    ctail = asb("ctail", [128, KI * 3], BF16)
    silu_c = asb("silu_c", [128, KD], F32)
    csb = asb("csb", [128, KD], F32)
    Wx_sb = asb("Wx_sb", [128, KI * 32], BF16)
    uvn = asb("uvn", [128, 4], F32)
    lns = asb("lns", [128, 8], F32)
    tks = asb("tks", [128, 24], F32)
    srow = asb("srow", [1, 512], F32)
    constt = asb("constt", [128, 2], F32)
    ssm_bf = asb("ssm_bf", [32, CH], BF16)
    convacc = asb("convacc", [128, CH], F32)
    Zbuf = asb("Zbuf", [128, CH], BF16)
    yb = asb("yb", [128, CH], F32)
    rW_sb = asb("rW_sb", [128, KD * NB], BF16)
    U_sb = asb("U_sb", [128, KD * NB], BF16)

    x_tok = asb("x_tok", [128, NTT * D], F32)
    lnA = asb("lnA", [128, D], F32)
    lnB = asb("lnB", [128, D], F32)
    wbufA = asb("wbufA", [128, 2 * 4096], BF16)
    R1 = asb("R1", [128, KI * (3 + CH)], BF16)
    R2 = asb("R2", [128, KI * CH], BF16)
    R3 = asb("R3", [128, KI * CH], BF16)
    R4 = asb("R4", [128, 2 * NS * CH], BF16)
    R5 = asb("R5", [128, NS * CH], BF16)
    W_slab = asb("W_slab", [128, NS * CH], BF16)

    psum = [es.enter_context(nc.psum_tensor(f"ps{i}", [128, 512], F32)) for i in range(8)]

    lnsq = lnA
    gam_rep = lnA
    t1f = yb
    x_main = R1
    ytot = R1[:, 0:KI * CH]
    x2tok = R1.bitcast(F32)[:, 0:NTT * D]
    x_conv = R2
    X1tok = R2.bitcast(F32)[:, 0:NTT * D]
    gate_s = R3
    hff = R3[:, 0:KD * CH]
    xn4f = R3[:, KD * CH:2 * KD * CH]
    prep_stage = R4.bitcast(F32)[:, 0:4096]
    prep_bf = R4[:, 2 * 4096:3 * 4096]
    B_rep = R4[:, 0:NS * CH]
    C_rep = R4[:, NS * CH:]
    g1 = R4
    xn2f = R5[:, 0:KD * CH]
    H_slab = R5
    r_tok = R5.bitcast(F32)[:, 0:512]
    hlat = R5.bitcast(F32)[:, 512:1024]
    alphaf = R5.bitcast(F32)[:, 1024:1536]
    zap = R5.bitcast(F32)[:, 1536:2048]
    G_tok = R5.bitcast(F32)[:, 2048:2560]
    Gf = R5[:, 5120:5120 + KB * CH]

    P = Prog(nc, es)
    B = {}

    def buf(name):
        if name not in B:
            B[name] = Buf(name)
        return B[name]

    pb = [buf(f"ps{i}") for i in range(8)]

    def dma(dst, src, reads=(), writes=(), q="sync", slow=False):
        if slow:
            return P.op(q, lambda e, d=dst, s=src: e.dma_start(
                out=d, in_=s, allow_slow_non_contiguous=True),
                reads=list(reads), writes=list(writes), dma=True)
        return P.op(q, lambda e, d=dst, s=src: e.dma_start(out=d, in_=s),
                    reads=list(reads), writes=list(writes), dma=True)

    def vop(fn, reads=(), writes=(), w=0):
        return P.op("vector", fn, reads=list(reads), writes=list(writes), w=w)

    def sop(fn, reads=(), writes=(), w=0):
        return P.op("scalar", fn, reads=list(reads), writes=list(writes), w=w)

    def top(fn, reads=(), writes=()):
        return P.op("tensor", fn, reads=list(reads), writes=list(writes))

    def gop(fn, reads=(), writes=()):
        return P.op("gpsimd", fn, reads=list(reads), writes=list(writes))

    gop(lambda e: e.iota(ident32[:], pattern=[[1, 128]], base=0,
                         channel_multiplier=0, allow_small_or_imprecise_dtypes=True),
        writes=[buf("id")])
    gop(lambda e: e.iota(pidx[:], pattern=[[0, 128]], base=0,
                         channel_multiplier=1, allow_small_or_imprecise_dtypes=True),
        writes=[buf("pidx")])
    vop(lambda e: e.tensor_tensor(out=ident32[:], in0=ident32[:], in1=pidx[:], op=ALU.is_equal),
        reads=[buf("pidx")], writes=[buf("id")])
    vop(lambda e: e.tensor_copy(out=ident16[:], in_=ident32[:]),
        reads=[buf("id")], writes=[buf("id16")])
    sop(lambda e: e.memzero(ones32[:]), writes=[buf("o32")])
    sop(lambda e: e.add(out=ones32[:], in_=ones32[:], add=1.0),
        reads=[buf("o32")], writes=[buf("o32")])
    vop(lambda e: e.tensor_copy(out=ones16[:], in_=ones32[:]),
        reads=[buf("o32")], writes=[buf("o16")])
    vop(lambda e: e.memset(carry[:], 0.0), writes=[buf("carry")])
    vop(lambda e: e.memset(constt[:, 0:1], LN_EPS), writes=[buf("constt")])
    nc.const_aps.register(F32, LN_EPS, constt[:, 0:1])
    vop(lambda e: e.memset(SEL[:], 0.0), writes=[buf("SEL")])
    for n in range(32):
        dma(SEL[n:n + 1, ts(n)], ones16[0:1, :], reads=[buf("o16")], writes=[buf("SEL")],
            q="gpsimd")

    def colload(dst, src1d, bname, n=128):
        dma(dst, src1d.rearrange("(a b) -> b a", b=n), writes=[buf(bname)], slow=True)

    colload(csb[:, 0:KD], c_row_d, "csb")
    sop(lambda e: e.activation(out=silu_c[:], in_=csb[:, 0:KD], func=AF.Silu),
        reads=[buf("csb")], writes=[buf("siluc")])
    colload(lnw_col[:, 0:KD], ln_w_d, "lnw")
    colload(sgn8[:, 0:KD], ln_b_d, "sgn")
    vop(lambda e: e.tensor_copy(out=lnb16[:], in_=sgn8[:, 0:KD]),
        reads=[buf("sgn")], writes=[buf("lnb16")])
    colload(lng_col[:, 0:KD], ln_g_d, "lng")
    colload(sgn8[:, 0:KD], ln_b4_d, "sgn")
    vop(lambda e: e.tensor_copy(out=lnb4_16[:], in_=sgn8[:, 0:KD]),
        reads=[buf("sgn")], writes=[buf("lnb4_16")])
    colload(bout_col[:, 0:KD], b_out_d, "bout")
    colload(btin_col[:], b_in_d, "btin")
    colload(convb_col[:], conv_b_d, "convb")
    colload(D_col[:], D_vec_d, "Dcol")
    for i_ in range(KI):
        dma(convw_col[:, i_ * 4:(i_ + 1) * 4], conv_w_d[ts(i_), :], writes=[buf("convw")])
    dma(bx_col[:], b_x_d.rearrange("(a b) -> b a", b=32), writes=[buf("bx")], slow=True)

    for ti in range(KI):
        dma(A_sb[:, ts(ti, NS)], A_log_d[ts(ti), :], writes=[buf("A")])
    sop(lambda e: e.activation(out=A_sb[:], in_=A_sb[:], func=AF.Exp),
        reads=[buf("A")], writes=[buf("A")])
    vop(lambda e: e.tensor_scalar(out=A_sb[:], in0=A_sb[:], scalar1=-1.0, scalar2=None,
                                  op0=ALU.mult),
        reads=[buf("A")], writes=[buf("A")], w=256)
    sop(lambda e: e.activation(out=A_sb[:], in_=A_sb[:], func=AF.Exp),
        reads=[buf("A")], writes=[buf("A")])

    for h in range(2):
        dma(srow[:], gamma_d[ts(h, 512)].unsqueeze(0), writes=[buf("srow")])
        top(lambda e: e.matmul(out=psum[5][:, :], lhsT=ones32[0:1, :],
                               rhs=srow[:], start=True, stop=True),
            reads=[buf("srow"), buf("o32")], writes=[pb[5]])
        sop(lambda e, h=h: e.copy(out=gam_rep[:, ts(h, 512)], in_=psum[5][:, :]),
            reads=[pb[5]], writes=[buf("lnA")])

    def rep_target(j):
        return {0: (sa_rep, "sarep", 0.0), 1: (ca1_rep, "carep", 1.0),
                3: (sb_rep, "sbrep", 0.0), 4: (cb1_rep, "cbrep", 1.0)}.get(j)

    for nsl in range(12):
        j, h = nsl // 2, nsl % 2
        dma(prep_stage.rearrange("p (k n) -> p k n", k=KD),
            wmod_d[:, ts(nsl, 512)].rearrange("(k p) n -> p k n", p=128),
            writes=[buf("R4")])
        for k in range(KD):
            top(lambda e, k=k: e.matmul(
                out=psum[6][0:1, :], lhsT=silu_c[:, k:k + 1],
                rhs=prep_stage[:, ts(k, 512)],
                start=(k == 0), stop=(k == KD - 1)),
                reads=[buf("R4"), buf("siluc")], writes=[pb[6]])
        sop(lambda e: e.copy(out=srow[:], in_=psum[6][0:1, :]),
            reads=[pb[6]], writes=[buf("srow")])
        tgt = rep_target(j)
        if tgt is not None:
            dst, dbn, bias = tgt
            top(lambda e: e.matmul(out=psum[5][:, :], lhsT=ones32[0:1, :], rhs=srow[:],
                                   start=True, stop=True),
                reads=[buf("srow"), buf("o32")], writes=[pb[5]])
            if bias == 0.0:
                sop(lambda e, dst=dst, h=h: e.activation(
                    out=dst[:, ts(h, 512)], in_=psum[5][:, :], func=AF.Copy),
                    reads=[pb[5]], writes=[buf(dbn)])
            else:
                sop(lambda e, dst=dst, h=h: e.activation(
                    out=dst[:, ts(h, 512)], in_=psum[5][:, :], func=AF.Copy, bias=1.0),
                    reads=[pb[5], buf("constt")], writes=[buf(dbn)])
        elif j in (2, 5):
            dma(mod_scr[0 if j == 2 else 1, ts(h, 512)].unsqueeze(0), srow[:],
                reads=[buf("srow")], writes=[buf("modscr")], q="gpsimd")
        if "tp_modrow" in taps:
            dma(taps["tp_modrow"][j, ts(h, 512)].unsqueeze(0), srow[:],
                reads=[buf("srow")], writes=[buf("tapm")])

    def clamp(col, cb_):
        vop(lambda e: e.tensor_scalar(out=sgn8[:], in0=col[:], scalar1=0.0, scalar2=2.0,
                                      op0=ALU.is_ge, op1=ALU.mult),
            reads=[cb_], writes=[buf("sgn")])
        vop(lambda e: e.tensor_scalar(out=sgn8[:], in0=sgn8[:], scalar1=-1.0, scalar2=None,
                                      op0=ALU.add),
            reads=[buf("sgn")], writes=[buf("sgn")])
        vop(lambda e: e.tensor_tensor(out=col[:], in0=col[:], in1=sgn8[:], op=ALU.mult),
            reads=[buf("sgn"), cb_], writes=[cb_])
        vop(lambda e: e.tensor_scalar(out=col[:], in0=col[:], scalar1=0.25, scalar2=None,
                                      op0=ALU.max),
            reads=[cb_], writes=[cb_])
        vop(lambda e: e.tensor_tensor(out=col[:], in0=col[:], in1=sgn8[:], op=ALU.mult),
            reads=[buf("sgn"), cb_], writes=[cb_])

    for j in (0, 1):
        for h in range(2):
            dma(srow[:], mod_scr[j, ts(h, 512)].unsqueeze(0),
                reads=[buf("modscr")], writes=[buf("srow")])
            vop(lambda e: e.tensor_scalar(out=btr32[:], in0=srow[:], scalar1=0.0,
                                          scalar2=2.0, op0=ALU.is_ge, op1=ALU.mult),
                reads=[buf("srow")], writes=[buf("btr32")])
            vop(lambda e: e.tensor_scalar(out=btr32[:], in0=btr32[:], scalar1=-1.0,
                                          scalar2=None, op0=ALU.add),
                reads=[buf("sgnr")], writes=[buf("btr32")])
            vop(lambda e: e.tensor_tensor(out=srow[:], in0=srow[:], in1=btr32[:],
                                          op=ALU.mult),
                reads=[buf("btr32"), buf("srow")], writes=[buf("srow")])
            vop(lambda e: e.tensor_scalar(out=srow[:], in0=srow[:], scalar1=0.25,
                                          scalar2=None, op0=ALU.max),
                reads=[buf("srow")], writes=[buf("srow")])
            vop(lambda e: e.tensor_tensor(out=srow[:], in0=srow[:], in1=btr32[:],
                                          op=ALU.mult),
                reads=[buf("btr32"), buf("srow")], writes=[buf("srow")])
            dma(mod_scr[j, ts(h, 512)].unsqueeze(0), srow[:],
                reads=[buf("srow")], writes=[buf("modscr")], q="gpsimd")
            if j == 0:
                top(lambda e: e.matmul(out=psum[5][:, :], lhsT=ones32[0:1, :],
                                       rhs=srow[:], start=True, stop=True),
                    reads=[buf("srow"), buf("o32")], writes=[pb[5]])
                sop(lambda e, h=h: e.copy(out=ga_rep[:, ts(h, 512)], in_=psum[5][:, :]),
                    reads=[pb[5]], writes=[buf("garep")])
    dma(ga_col[:, 0:KD], mod_scr[0].rearrange("(a b) -> b a", b=128),
        reads=[buf("modscr")], writes=[buf("ga")], slow=True)
    dma(gb_col[:, 0:KD], mod_scr[1].rearrange("(a b) -> b a", b=128),
        reads=[buf("modscr")], writes=[buf("gb")], slow=True)
    vop(lambda e: e.tensor_tensor(out=gabo_col[:], in0=ga_col[:], in1=bout_col[:, 0:KD],
                                  op=ALU.mult),
        reads=[buf("ga"), buf("bout")], writes=[buf("gabo")])

    vop(lambda e: e.memset(prep_bf[:, 0:512], 0.0), writes=[buf("R4")])
    top(lambda e: e.matmul(out=psum[7][:, :], lhsT=prep_bf[:, 0:128],
                           rhs=prep_bf[:, 0:512], start=True, stop=False),
        reads=[buf("R4")], writes=[pb[7]])
    for k in range(KD):
        dma(prep_stage[:], W_in_d[ts(k), :], writes=[buf("R4")])
        vop(lambda e: e.tensor_copy(out=prep_bf[:], in_=prep_stage[:]),
            reads=[buf("R4")], writes=[buf("R4")], w=4096)
        for o in range(32):
            top(lambda e, k=k, o=o: e.matmul(
                out=psum[7][:, o:o + 1], lhsT=prep_bf[:, ts(o)],
                rhs=lnb16[:, k:k + 1],
                start=False, stop=(k == KD - 1)),
                reads=[buf("R4"), buf("lnb16")], writes=[pb[7]])
        vop(lambda e, k=k: e.tensor_scalar(out=prep_bf[:], in0=prep_stage[:],
                                           scalar1=lnw_col[:, k:k + 1], scalar2=None,
                                           op0=ALU.mult),
            reads=[buf("R4"), buf("lnw")], writes=[buf("R4")], w=4096)
        for p in range(8):
            dma(Win_scr[p, :, ts(k, 512)], prep_bf[:, ts(p, 512)],
                reads=[buf("R4")], writes=[buf("winscr")], q="gpsimd")
    vop(lambda e: e.tensor_tensor(out=btin_col[:], in0=btin_col[:], in1=psum[7][:, 0:32],
                                  op=ALU.add),
        reads=[pb[7], buf("btin")], writes=[buf("btin")])

    for i in range(KI):
        dma(prep_stage[:, 0:1024], W_out_d[ts(i), :], writes=[buf("R4")])
        vop(lambda e: e.tensor_copy(out=prep_bf[:, 0:1024], in_=prep_stage[:, 0:1024]),
            reads=[buf("R4")], writes=[buf("R4")])
        for g in range(4):
            dma(Wout_scr[g, :, ts(i, 256)], prep_bf[:, g * 256:(g + 1) * 256],
                reads=[buf("R4")], writes=[buf("woscr")], q="gpsimd")

    for i in range(KI):
        dma(prep_stage[:, 0:32], W_x_d[ts(i), :], writes=[buf("R4")])
        vop(lambda e, i=i: e.tensor_copy(out=Wx_sb[:, i * 32:(i + 1) * 32],
                                         in_=prep_stage[:, 0:32]),
            reads=[buf("R4")], writes=[buf("Wx")])

    for r in range(KH):
        dma(prep_stage[:, 0:D], W1_d[ts(r), :], writes=[buf("R4")])
        for cc in range(KD):
            bank = cc % 2
            top(lambda e, cc=cc, bank=bank: e.transpose(
                out=psum[bank][:, 0:128], in_=prep_stage[:, ts(cc)], identity=ident32[:]),
                reads=[buf("R4"), buf("id")], writes=[pb[bank]])
            sop(lambda e, cc=cc, bank=bank: e.copy(
                out=prep_bf[:, cc * 128: cc * 128 + 128], in_=psum[bank][:, 0:128]),
                reads=[pb[bank]], writes=[buf("R4")])
        for cc in range(KD):
            dma(W1_scr[(r * 128 % 4096) // 512, :,
                       cc * 512 + (r * 128) % 512: cc * 512 + (r * 128) % 512 + 128],
                prep_bf[:, cc * 128: cc * 128 + 128],
                reads=[buf("R4")], writes=[buf("w1scr")], q="gpsimd")

    for r in range(KD):
        dma(prep_stage[:], W2_d[ts(r), :], writes=[buf("R4")])
        for q8 in range(4):
            for jj in range(8):
                cc = q8 * 8 + jj
                bank = jj % 2
                top(lambda e, cc=cc, bank=bank: e.transpose(
                    out=psum[bank][:, 0:128], in_=prep_stage[:, ts(cc)],
                    identity=ident32[:]),
                    reads=[buf("R4"), buf("id")], writes=[pb[bank]])
                sop(lambda e, jj=jj, bank=bank: e.copy(
                    out=prep_bf[:, jj * 128: jj * 128 + 128], in_=psum[bank][:, 0:128]),
                    reads=[pb[bank]], writes=[buf("R4")])
            for jj in range(8):
                cc = q8 * 8 + jj
                dma(W2_scr[cc // 4, :, (cc % 4) * 1024 + r * 128:
                          (cc % 4) * 1024 + r * 128 + 128],
                    prep_bf[:, jj * 128: jj * 128 + 128],
                    reads=[buf("R4")], writes=[buf("w2scr")], q="gpsimd")

    for r in range(KB):
        dma(prep_stage[:, 0:D], router_W_d[ts(r), :], writes=[buf("R4")])
        for cc in range(KD):
            bank = cc % 2
            top(lambda e, cc=cc, bank=bank: e.transpose(
                out=psum[bank][:, 0:128], in_=prep_stage[:, ts(cc)], identity=ident32[:]),
                reads=[buf("R4"), buf("id")], writes=[pb[bank]])
            sop(lambda e, cc=cc, r=r, bank=bank: e.activation(
                out=rW_sb[:, cc * NB + r * 128: cc * NB + r * 128 + 128],
                in_=psum[bank][:, 0:128], func=AF.Identity, scale=lng_col[:, cc:cc + 1]),
                reads=[pb[bank], buf("lng")], writes=[buf("rW")])
    dma(btr32[:], router_b_d.unsqueeze(0), writes=[buf("btr32")])
    for cc in range(KD):
        top(lambda e, cc=cc: e.matmul(out=psum[6][0:1, :],
                                      lhsT=lnb4_16[:, cc:cc + 1],
                                      rhs=rW_sb[:, cc * NB:(cc + 1) * NB],
                                      start=(cc == 0), stop=(cc == KD - 1)),
            reads=[buf("rW"), buf("lnb4_16")], writes=[pb[6]])
    vop(lambda e: e.tensor_tensor(out=btr32[:], in0=btr32[:], in1=psum[6][0:1, :],
                                  op=ALU.add),
        reads=[pb[6], buf("btr32")], writes=[buf("btr32")])
    vop(lambda e: e.tensor_copy(out=btr_row[:], in_=btr32[:]),
        reads=[buf("btr32")], writes=[buf("btr")])

    for (srcd, isU) in ((raw_U_d, True), (raw_V_d, False)):
        for r in range(KB):
            dma(prep_stage[:, 0:D], srcd[ts(r), :], writes=[buf("R4")])
            sop(lambda e: e.activation(out=lnB[:], in_=prep_stage[:, 0:D], func=AF.Square),
                reads=[buf("R4")], writes=[buf("lnB")])
            vop(lambda e: e.reduce_sum(out=uvn[:, 0:1], in_=lnB[:], axis=AX),
                reads=[buf("lnB")], writes=[buf("uvn")])
            sop(lambda e: e.activation(out=uvn[:, 1:2], in_=uvn[:, 0:1], func=AF.Sqrt),
                reads=[buf("uvn")], writes=[buf("uvn")])
            vop(lambda e: e.tensor_scalar(out=uvn[:, 1:2], in0=uvn[:, 1:2], scalar1=FEPS,
                                          scalar2=None, op0=ALU.max),
                reads=[buf("uvn")], writes=[buf("uvn")])
            vop(lambda e: e.reciprocal(out=uvn[:, 2:3], in_=uvn[:, 1:2]),
                reads=[buf("uvn")], writes=[buf("uvn")])
            vop(lambda e: e.tensor_scalar(out=prep_stage[:, 0:D], in0=prep_stage[:, 0:D],
                                          scalar1=uvn[:, 2:3], scalar2=None, op0=ALU.mult),
                reads=[buf("uvn")], writes=[buf("R4")])
            if isU:
                for cc in range(KD):
                    bank = cc % 2
                    top(lambda e, cc=cc, bank=bank: e.transpose(
                        out=psum[bank][:, 0:128], in_=prep_stage[:, ts(cc)],
                        identity=ident32[:]),
                        reads=[buf("R4"), buf("id")], writes=[pb[bank]])
                    sop(lambda e, cc=cc, r=r, bank=bank: e.copy(
                        out=U_sb[:, cc * NB + r * 128: cc * NB + r * 128 + 128],
                        in_=psum[bank][:, 0:128]),
                        reads=[pb[bank]], writes=[buf("U")])
            else:
                vop(lambda e: e.tensor_tensor(
                    out=prep_bf[:, 0:D], in0=prep_stage[:, 0:D],
                    in1=gam_rep[:], op=ALU.mult),
                    reads=[buf("R4"), buf("lnA")], writes=[buf("R4")])
                dma(V_scr[r], prep_bf[:, 0:D], reads=[buf("R4")], writes=[buf("vscr")],
                    q="gpsimd")

    if "tp_gar" in taps:
        dma(taps["tp_gar"][:], ga_rep[:], reads=[buf("garep")], writes=[buf("tapgr")])
    if "tp_prep" in taps:
        for nm, tsr, bn in (("ca", ca1_rep, "carep"), ("sa", sa_rep, "sarep"),
                            ("cb", cb1_rep, "cbrep"), ("sb", sb_rep, "sbrep"),
                            ("gar", ga_rep, "garep")):
            off = {"ca": 0, "sa": 1, "cb": 2, "sb": 3, "gar": 4}[nm] * D
            dma(taps["tp_prep"][:, off:off + D], tsr[:], reads=[buf(bn)],
                writes=[buf("tapp")])
    if "tp_sel" in taps:
        dma(taps["tp_sel"][:], SEL[:], reads=[buf("SEL")], writes=[buf("taps")])
    if "tp_asb" in taps:
        dma(taps["tp_asb"][:], A_sb[:], reads=[buf("A")], writes=[buf("tapa")])
    if "tp_misc" in taps:
        for nm, tsr, bn, off, wdt in (
                ("ga", ga_col, "ga", 0, KD), ("gb", gb_col, "gb", 8, KD),
                ("gabo", gabo_col, "gabo", 16, KD), ("btin", btin_col, "btin", 24, 32),
                ("cw", convw_col, "convw", 56, KI * 4), ("cb2", convb_col, "convb", 120, KI),
                ("dc", D_col, "Dcol", 136, KI)):
            dma(taps["tp_misc"][:, off:off + wdt], tsr[:, 0:wdt], reads=[buf(bn)],
                writes=[buf("tapm2")])
    if "tp_gacol" in taps:
        dma(taps["tp_gacol"][:], ga_col[:], reads=[buf("ga")], writes=[buf("tapgc")])

    inv_d = 1.0 / D

    def ln_stats(src_ap, src_buf):
        sop(lambda e: e.activation(out=lnsq[:], in_=src_ap, func=AF.Square),
            reads=[src_buf], writes=[buf("lnA")])
        vop(lambda e: e.reduce_sum(out=lns[:, 0:1], in_=src_ap, axis=AX),
            reads=[src_buf], writes=[buf("lns")])
        vop(lambda e: e.reduce_sum(out=lns[:, 1:2], in_=lnsq[:], axis=AX),
            reads=[buf("lnA")], writes=[buf("lns")])
        vop(lambda e: e.tensor_scalar(out=lns[:, 2:3], in0=lns[:, 0:1], scalar1=inv_d,
                                      scalar2=None, op0=ALU.mult),
            reads=[buf("lns")], writes=[buf("lns")])
        vop(lambda e: e.tensor_scalar(out=lns[:, 3:4], in0=lns[:, 1:2], scalar1=inv_d,
                                      scalar2=None, op0=ALU.mult),
            reads=[buf("lns")], writes=[buf("lns")])
        vop(lambda e: e.tensor_scalar(out=lns[:, 4:5], in0=lns[:, 2:3],
                                      scalar1=lns[:, 2:3], scalar2=None, op0=ALU.mult),
            reads=[buf("lns")], writes=[buf("lns")])
        vop(lambda e: e.tensor_tensor(out=lns[:, 4:5], in0=lns[:, 3:4], in1=lns[:, 4:5],
                                      op=ALU.subtract),
            reads=[buf("lns")], writes=[buf("lns")])
        sop(lambda e: e.activation(out=lns[:, 4:5], in_=lns[:, 4:5], func=AF.Sqrt,
                                   bias=LN_EPS),
            reads=[buf("lns"), buf("constt")], writes=[buf("lns")])
        vop(lambda e: e.reciprocal(out=lns[:, 5:6], in_=lns[:, 4:5]),
            reads=[buf("lns")], writes=[buf("lns")])
        vop(lambda e: e.tensor_scalar(out=lns[:, 6:7], in0=lns[:, 2:3],
                                      scalar1=lns[:, 5:6], scalar2=-1.0,
                                      op0=ALU.mult, op1=ALU.mult),
            reads=[buf("lns")], writes=[buf("lns")])

    def ln_apply(dst_ap, dst_buf, src_ap, src_buf):
        vop(lambda e: e.tensor_scalar(out=dst_ap, in0=src_ap, scalar1=lns[:, 5:6],
                                      scalar2=lns[:, 6:7], op0=ALU.mult, op1=ALU.add),
            reads=[src_buf, buf("lns")], writes=[dst_buf], w=1024)

    def transpose_to(dst_ap_fn, dst_buf, src_ap, src_buf, ident, ident_buf, nk,
                     banks=(0, 1)):
        for k in range(nk):
            bank = banks[k % len(banks)]
            col = (k // len(banks)) % 4 if len(banks) == 1 else 0
            top(lambda e, k=k, bank=bank, col=col: e.transpose(
                out=psum[bank][:, col * 128: col * 128 + 128], in_=src_ap[:, ts(k)],
                identity=ident[:]),
                reads=[src_buf, ident_buf], writes=[pb[bank]])
            sop(lambda e, k=k, bank=bank, col=col: e.copy(
                out=dst_ap_fn(k), in_=psum[bank][:, col * 128: col * 128 + 128]),
                reads=[pb[bank]], writes=[dst_buf])

    out_bufs = []

    for c in range(NCH):
        t0 = c * CH
        for tt in range(NTT):
            dma(x_tok[:, ts(tt, D)], X_in[slice(t0 + tt * 128, t0 + (tt + 1) * 128), :],
                writes=[buf("x_tok")])

        for tt in range(NTT):
            xs = x_tok[:, ts(tt, D)]
            ln_stats(xs, buf("x_tok"))
            ln_apply(lnB[:], buf("lnB"), xs, buf("x_tok"))
            vop(lambda e: e.tensor_tensor(out=lnB[:], in0=lnB[:], in1=ca1_rep[:],
                                          op=ALU.mult),
                reads=[buf("carep")], writes=[buf("lnB")], w=1024)
            vop(lambda e: e.tensor_tensor(out=lnB[:], in0=lnB[:], in1=sa_rep[:],
                                          op=ALU.add),
                reads=[buf("sarep")], writes=[buf("lnB")], w=1024)
            if "tp_h1" in taps:
                dma(taps["tp_h1"][slice(t0 + tt * 128, t0 + tt * 128 + 128), :], lnB[:],
                    reads=[buf("lnB")], writes=[buf("tapg")])
            vop(lambda e: e.tensor_tensor(out=lnA[:], in0=lnB[:], in1=ga_rep[:],
                                          op=ALU.mult),
                reads=[buf("lnB"), buf("garep")], writes=[buf("lnA")], w=1024)
            vop(lambda e, tt=tt: e.tensor_tensor(out=x_tok[:, ts(tt, D)],
                                                 in0=x_tok[:, ts(tt, D)], in1=lnA[:],
                                                 op=ALU.add),
                reads=[buf("lnA")], writes=[buf("x_tok")], w=1024)
            if "tp_xup" in taps:
                dma(taps["tp_xup"][slice(t0 + tt * 128, t0 + tt * 128 + 128), :],
                    x_tok[:, ts(tt, D)], reads=[buf("x_tok")], writes=[buf("tapxu")])
            ln_stats(lnB[:], buf("lnB"))
            ln_apply(lnA[:], buf("lnA"), lnB[:], buf("lnB"))
            transpose_to(lambda k, tt=tt: xn2f[:, k * CH + tt * 128: k * CH + tt * 128 + 128],
                         buf("R5"), lnA, buf("lnA"), ident32, buf("id"), KD)
            if "tp_xn1" in taps:
                dma(taps["tp_xn1"][slice(t0 + tt * 128, t0 + tt * 128 + 128), :], lnA[:],
                    reads=[buf("lnA")], writes=[buf("tapf")])

        if "tp_xn2f" in taps:
            for k in range(KD):
                dma(taps["tp_xn2f"][ts(k), t0:t0 + CH], xn2f[:, ts(k, CH)],
                    reads=[buf("R5")], writes=[buf("tap5b")])
        for i in range(KI):
            if c == 0:
                vop(lambda e, i=i: e.memset(x_main[:, i * 515: i * 515 + 3], 0.0),
                    writes=[buf("R1")])
            else:
                vop(lambda e, i=i: e.tensor_copy(out=x_main[:, i * 515: i * 515 + 3],
                                                 in_=ctail[:, i * 3:(i + 1) * 3]),
                    reads=[buf("ctail")], writes=[buf("R1")])
        for og in range(8):
            hbase = (og % 2) * 4096
            wb = buf(f"wA{og % 2}")
            dma(wbufA[:, hbase:hbase + 4096], Win_scr[og], reads=[buf("winscr")],
                writes=[wb])
            for oi in range(4):
                o = og * 4 + oi
                bank = 2 + (o % 2)
                for k in range(KD):
                    top(lambda e, k=k, oi=oi, bank=bank, hbase=hbase: e.matmul(
                        out=psum[bank][:, :],
                        lhsT=wbufA[:, hbase + k * 512 + oi * 128: hbase + k * 512 + oi * 128 + 128],
                        rhs=xn2f[:, ts(k, CH)], start=(k == 0), stop=(k == KD - 1)),
                        reads=[wb, buf("R5")], writes=[pb[bank]])
                if o < 16:
                    sop(lambda e, o=o, bank=bank: e.activation(
                        out=x_main[:, o * 515 + 3: o * 515 + 3 + CH],
                        in_=psum[bank][:, :], func=AF.Identity, bias=btin_col[:, o:o + 1]),
                        reads=[pb[bank], buf("btin")], writes=[buf("R1")])
                else:
                    sop(lambda e, o=o, bank=bank: e.activation(
                        out=gate_s[:, (o - 16) * CH: (o - 15) * CH],
                        in_=psum[bank][:, :], func=AF.Silu, bias=btin_col[:, o:o + 1]),
                        reads=[pb[bank], buf("btin")], writes=[buf("R3")])

        for i in range(KI):
            base = i * 515
            vop(lambda e, base=base, i=i: e.tensor_scalar(
                out=convacc[:], in0=x_main[:, base: base + CH],
                scalar1=convw_col[:, i * 4: i * 4 + 1], scalar2=None, op0=ALU.mult),
                reads=[buf("R1"), buf("convw")], writes=[buf("cacc")], w=CH)
            for kk in range(1, 4):
                vop(lambda e, base=base, i=i, kk=kk: e.scalar_tensor_tensor(
                    out=convacc[:], in0=x_main[:, base + kk: base + kk + CH],
                    scalar=convw_col[:, i * 4 + kk: i * 4 + kk + 1], in1=convacc[:],
                    op0=ALU.mult, op1=ALU.add),
                    reads=[buf("R1"), buf("convw"), buf("cacc")], writes=[buf("cacc")], w=CH)
            vop(lambda e, i=i: e.tensor_copy(out=ctail[:, i * 3:(i + 1) * 3],
                                             in_=x_main[:, i * 515 + CH: i * 515 + CH + 3]),
                reads=[buf("R1")], writes=[buf("ctail")])
            sop(lambda e, i=i: e.activation(out=x_conv[:, ts(i, CH)], in_=convacc[:],
                                            func=AF.Silu, bias=convb_col[:, i:i + 1]),
                reads=[buf("cacc"), buf("convb")], writes=[buf("R2")])

        if "tp_xmain" in taps:
            for i in range(KI):
                dma(taps["tp_xmain"][ts(i), t0:t0 + CH],
                    x_main[:, i * 515 + 3: i * 515 + 3 + CH],
                    reads=[buf("R1")], writes=[buf("tap8")])
        if "tp_gate" in taps:
            for i in range(KI):
                dma(taps["tp_gate"][ts(i), t0:t0 + CH], gate_s[:, ts(i, CH)],
                    reads=[buf("R3")], writes=[buf("tap9")])
        for i in range(KI):
            top(lambda e, i=i: e.matmul(out=psum[4][0:32, :],
                                        lhsT=Wx_sb[:, i * 32:(i + 1) * 32],
                                        rhs=x_conv[:, ts(i, CH)],
                                        start=(i == 0), stop=(i == KI - 1)),
                reads=[buf("Wx"), buf("R2")], writes=[pb[4]])
        sop(lambda e: e.activation(out=ssm_bf[:], in_=psum[4][0:32, :], func=AF.Identity,
                                   bias=bx_col[:]),
            reads=[pb[4], buf("bx")], writes=[buf("ssm")])
        if "tp_ssm" in taps:
            dma(taps["tp_ssm"][:, t0:t0 + CH], ssm_bf[:], reads=[buf("ssm")],
                writes=[buf("tap1")])

        for n in range(2 * NS):
            bank = 5 + (n % 2)
            top(lambda e, n=n, bank=bank: e.matmul(
                out=psum[bank][:, :], lhsT=SEL[:, ts(n)], rhs=ssm_bf[:, :],
                start=True, stop=True),
                reads=[buf("ssm"), buf("SEL")], writes=[pb[bank]])
            dstt = B_rep if n < NS else C_rep
            nn = n % NS
            sop(lambda e, nn=nn, bank=bank, dstt=dstt: e.copy(
                out=dstt[:, ts(nn, CH)], in_=psum[bank][:, :]),
                reads=[pb[bank]], writes=[buf("R4")])

        if c == 0 and "tp_brep" in taps:
            dma(taps["tp_brep"][:], B_rep[:], reads=[buf("R4")], writes=[buf("tapbr")])
        if c == 0 and "tp_crep" in taps:
            dma(taps["tp_crep"][:], C_rep[:], reads=[buf("R4")], writes=[buf("tapcr")])
        for i in range(KI):
            for n in range(NS):
                vop(lambda e, i=i, n=n: e.tensor_tensor(
                    out=Zbuf[:], in0=x_conv[:, ts(i, CH)], in1=B_rep[:, ts(n, CH)],
                    op=ALU.mult),
                    reads=[buf("R2"), buf("R4")], writes=[buf("Z")], w=CH)
                vop(lambda e, i=i, n=n: e.tensor_tensor_scan(
                    out=H_slab[:, ts(n, CH)],
                    data0=A_sb[:, i * NS + n: i * NS + n + 1].to_broadcast([128, CH]),
                    data1=Zbuf[:], initial=carry[:, i * NS + n: i * NS + n + 1],
                    op0=ALU.mult, op1=ALU.add),
                    reads=[buf("Z"), buf("A"), buf("carry")], writes=[buf("R5")], w=CH)
            if c == 0 and i == 0 and "tp_H0" in taps:
                dma(taps["tp_H0"][:], H_slab[:], reads=[buf("R5")], writes=[buf("tapH")])
            vop(lambda e, i=i: e.tensor_copy(out=carry[:, ts(i, NS)],
                                             in_=H_slab[:, CH - 1::CH]),
                reads=[buf("R5")], writes=[buf("carry")])
            vop(lambda e: e.tensor_tensor(out=W_slab[:], in0=H_slab[:], in1=C_rep[:],
                                          op=ALU.mult),
                reads=[buf("R5"), buf("R4")], writes=[buf("Wsl")], w=NS * CH)
            vop(lambda e: e.tensor_tensor(out=W_slab[:, 0:8 * CH], in0=W_slab[:, 0:8 * CH],
                                          in1=W_slab[:, 8 * CH:], op=ALU.add),
                writes=[buf("Wsl")], w=8 * CH)
            vop(lambda e: e.tensor_tensor(out=W_slab[:, 0:4 * CH], in0=W_slab[:, 0:4 * CH],
                                          in1=W_slab[:, 4 * CH:8 * CH], op=ALU.add),
                writes=[buf("Wsl")], w=4 * CH)
            vop(lambda e: e.tensor_tensor(out=W_slab[:, 0:2 * CH], in0=W_slab[:, 0:2 * CH],
                                          in1=W_slab[:, 2 * CH:4 * CH], op=ALU.add),
                writes=[buf("Wsl")], w=2 * CH)
            vop(lambda e: e.tensor_tensor(out=yb[:], in0=W_slab[:, 0:CH],
                                          in1=W_slab[:, CH:2 * CH], op=ALU.add),
                reads=[buf("Wsl")], writes=[buf("yb")], w=CH)
            vop(lambda e, i=i: e.scalar_tensor_tensor(
                out=convacc[:], in0=x_conv[:, ts(i, CH)], scalar=D_col[:, i:i + 1],
                in1=yb[:], op0=ALU.mult, op1=ALU.add),
                reads=[buf("R2"), buf("Dcol"), buf("yb")], writes=[buf("cacc")], w=CH)
            vop(lambda e, i=i: e.tensor_tensor(out=ytot[:, ts(i, CH)], in0=convacc[:],
                                               in1=gate_s[:, ts(i, CH)], op=ALU.mult),
                reads=[buf("cacc"), buf("R3")], writes=[buf("R1")], w=CH)
        if "tp_xconv" in taps:
            for i in range(KI):
                dma(taps["tp_xconv"][ts(i), t0:t0 + CH], x_conv[:, ts(i, CH)],
                    reads=[buf("R2")], writes=[buf("tap2")])
        if "tp_ytot" in taps:
            for i in range(KI):
                dma(taps["tp_ytot"][ts(i), t0:t0 + CH], ytot[:, ts(i, CH)],
                    reads=[buf("R1")], writes=[buf("tap3")])

        for g in range(4):
            hbase = (g % 2) * 4096
            wb = buf(f"wA{g % 2}")
            dma(wbufA[:, hbase:hbase + KI * 256], Wout_scr[g], reads=[buf("woscr")],
                writes=[wb])
            for dh in range(2):
                dt = g * 2 + dh
                bank = 2 + dh
                for i in range(KI):
                    top(lambda e, i=i, dh=dh, bank=bank, hbase=hbase: e.matmul(
                        out=psum[bank][:, :],
                        lhsT=wbufA[:, hbase + i * 256 + dh * 128: hbase + i * 256 + dh * 128 + 128],
                        rhs=ytot[:, ts(i, CH)], start=(i == 0), stop=(i == KI - 1)),
                        reads=[wb, buf("R1")], writes=[pb[bank]])
                sop(lambda e, dt=dt, bank=bank: e.activation(
                    out=t1f[:], in_=psum[bank][:, :], func=AF.Identity,
                    scale=ga_col[:, dt:dt + 1], bias=gabo_col[:, dt:dt + 1]),
                    reads=[pb[bank], buf("ga"), buf("gabo")], writes=[buf("yb")])
                transpose_to(lambda k, dt=dt: X1tok[:, k * D + dt * 128:
                                                    k * D + dt * 128 + 128],
                             buf("R2"), t1f, buf("yb"), ident32, buf("id"), NTT)
        for tt in range(NTT):
            vop(lambda e, tt=tt: e.tensor_tensor(
                out=X1tok[:, ts(tt, D)], in0=X1tok[:, ts(tt, D)], in1=x_tok[:, ts(tt, D)],
                op=ALU.add),
                reads=[buf("x_tok")], writes=[buf("R2")], w=1024)
        if "tp_X1" in taps:
            for tt in range(NTT):
                dma(taps["tp_X1"][slice(t0 + tt * 128, t0 + tt * 128 + 128), :],
                    X1tok[:, ts(tt, D)], reads=[buf("R2")], writes=[buf("tap4")])

        for tt in range(NTT):
            xs = X1tok[:, ts(tt, D)]
            ln_stats(xs, buf("R2"))
            ln_apply(lnB[:], buf("lnB"), xs, buf("R2"))
            vop(lambda e: e.tensor_tensor(out=lnB[:], in0=lnB[:], in1=cb1_rep[:],
                                          op=ALU.mult),
                reads=[buf("cbrep")], writes=[buf("lnB")], w=1024)
            vop(lambda e: e.tensor_tensor(out=lnB[:], in0=lnB[:], in1=sb_rep[:],
                                          op=ALU.add),
                reads=[buf("sbrep")], writes=[buf("lnB")], w=1024)
            transpose_to(lambda k, tt=tt: hff[:, k * CH + tt * 128: k * CH + tt * 128 + 128],
                         buf("R3"), lnB, buf("lnB"), ident32, buf("id"), KD)
            ln_stats(lnB[:], buf("lnB"))
            ln_apply(lnA[:], buf("lnA"), lnB[:], buf("lnB"))
            transpose_to(lambda k, tt=tt: xn4f[:, k * CH + tt * 128: k * CH + tt * 128 + 128],
                         buf("R3"), lnA, buf("lnA"), ident32, buf("id"), KD)
        if "tp_hff" in taps:
            for k in range(KD):
                dma(taps["tp_hff"][ts(k), t0:t0 + CH], hff[:, ts(k, CH)],
                    reads=[buf("R3")], writes=[buf("tap5")])
        for tt in range(NTT):
            for k in range(KD):
                top(lambda e, k=k, tt=tt: e.matmul(
                    out=psum[4][:, :],
                    lhsT=xn4f[:, k * CH + tt * 128: k * CH + tt * 128 + 128],
                    rhs=rW_sb[:, k * NB:(k + 1) * NB], start=(k == 0), stop=False),
                    reads=[buf("R3"), buf("rW")], writes=[pb[4]])
            top(lambda e: e.matmul(out=psum[4][:, :], lhsT=ones16[0:1, :], rhs=btr_row[:],
                                   start=False, stop=True),
                reads=[buf("btr"), buf("o16")], writes=[pb[4]])
            sop(lambda e: e.copy(out=r_tok[:], in_=psum[4][:, :]),
                reads=[pb[4]], writes=[buf("R5")])
            for k in range(KD):
                top(lambda e, k=k, tt=tt: e.matmul(
                    out=psum[5][:, :],
                    lhsT=hff[:, k * CH + tt * 128: k * CH + tt * 128 + 128],
                    rhs=U_sb[:, k * NB:(k + 1) * NB], start=(k == 0), stop=(k == KD - 1)),
                    reads=[buf("R3"), buf("U")], writes=[pb[5]])
            sop(lambda e: e.copy(out=hlat[:], in_=psum[5][:, :]),
                reads=[pb[5]], writes=[buf("R5")])
            vop(lambda e: e.max(out=tks[:, 0:8], in_=r_tok[:]),
                reads=[buf("R5")], writes=[buf("tks")])
            vop(lambda e: e.tensor_scalar(out=alphaf[:], in0=r_tok[:], scalar1=TAU,
                                          scalar2=-TAU, op0=ALU.min, op1=ALU.max),
                writes=[buf("R5")], w=512)
            sop(lambda e: e.activation(out=alphaf[:], in_=alphaf[:], func=AF.Exp),
                reads=[buf("R5")], writes=[buf("R5")], w=512)
            sop(lambda e: e.activation(out=alphaf[:], in_=alphaf[:], func=AF.Ln, bias=1.0),
                reads=[buf("R5"), buf("constt")], writes=[buf("R5")], w=512)
            vop(lambda e: e.match_replace(out=zap[:], in_to_replace=tks[:, 0:8],
                                          in_values=r_tok[:], imm_value=-1e30),
                reads=[buf("tks")], writes=[buf("R5")])
            vop(lambda e: e.tensor_scalar(out=tks[:, 8:16], in0=tks[:, 0:8], scalar1=TAU,
                                          scalar2=-TAU, op0=ALU.min, op1=ALU.max),
                reads=[buf("tks")], writes=[buf("tks")])
            sop(lambda e: e.activation(out=tks[:, 8:16], in_=tks[:, 8:16], func=AF.Exp),
                reads=[buf("tks")], writes=[buf("tks")])
            sop(lambda e: e.activation(out=tks[:, 8:16], in_=tks[:, 8:16], func=AF.Ln,
                                       bias=1.0),
                reads=[buf("tks"), buf("constt")], writes=[buf("tks")])
            vop(lambda e: e.reduce_sum(out=tks[:, 16:17], in_=tks[:, 8:16], axis=AX),
                reads=[buf("tks")], writes=[buf("tks")])
            sop(lambda e: e.activation(out=tks[:, 17:18], in_=tks[:, 16:17], func=AF.Tanh),
                reads=[buf("tks")], writes=[buf("tks")])
            vop(lambda e: e.tensor_scalar(out=tks[:, 18:19], in0=tks[:, 16:17],
                                          scalar1=FEPS, scalar2=None, op0=ALU.add),
                reads=[buf("tks")], writes=[buf("tks")])
            vop(lambda e: e.reciprocal(out=tks[:, 19:20], in_=tks[:, 18:19]),
                reads=[buf("tks")], writes=[buf("tks")])
            vop(lambda e: e.tensor_scalar(out=tks[:, 20:21], in0=tks[:, 17:18],
                                          scalar1=tks[:, 19:20], scalar2=None,
                                          op0=ALU.mult),
                reads=[buf("tks")], writes=[buf("tks")])
            vop(lambda e: e.tensor_tensor(out=zap[:], in0=r_tok[:], in1=zap[:],
                                          op=ALU.not_equal),
                writes=[buf("R5")], w=512)
            vop(lambda e: e.tensor_tensor(out=zap[:], in0=zap[:], in1=alphaf[:],
                                          op=ALU.mult),
                writes=[buf("R5")], w=512)
            vop(lambda e: e.tensor_tensor(out=zap[:], in0=zap[:], in1=hlat[:],
                                          op=ALU.mult),
                writes=[buf("R5")], w=512)
            vop(lambda e: e.tensor_scalar(out=G_tok[:], in0=zap[:],
                                          scalar1=tks[:, 20:21], scalar2=None,
                                          op0=ALU.mult),
                reads=[buf("tks")], writes=[buf("R5")], w=512)
            if "tp_r" in taps:
                dma(taps["tp_r"][slice(t0 + tt * 128, t0 + tt * 128 + 128), :], r_tok[:],
                    reads=[buf("R5")], writes=[buf("tap6")])
            if "tp_G" in taps:
                dma(taps["tp_G"][slice(t0 + tt * 128, t0 + tt * 128 + 128), :], G_tok[:],
                    reads=[buf("R5")], writes=[buf("tap7")])
            transpose_to(lambda k, tt=tt: Gf[:, k * CH + tt * 128: k * CH + tt * 128 + 128],
                         buf("R5"), G_tok, buf("R5"), ident32, buf("id"), KB,
                         banks=(6, 7))

        for og in range(8):
            hbase = (og % 2) * 4096
            wb = buf(f"wA{og % 2}")
            dma(wbufA[:, hbase:hbase + 4096], W1_scr[og], reads=[buf("w1scr")],
                writes=[wb], q="scalar")
            for oi in range(4):
                o = og * 4 + oi
                bank = 2 + (o % 2)
                for k in range(KD):
                    top(lambda e, k=k, oi=oi, bank=bank, hbase=hbase: e.matmul(
                        out=psum[bank][:, :],
                        lhsT=wbufA[:, hbase + k * 512 + oi * 128: hbase + k * 512 + oi * 128 + 128],
                        rhs=hff[:, ts(k, CH)], start=(k == 0), stop=(k == KD - 1)),
                        reads=[wb, buf("R3")], writes=[pb[bank]])
                sop(lambda e, o=o, bank=bank: e.activation(
                    out=g1[:, ts(o, CH)], in_=psum[bank][:, :], func=AF.Gelu),
                    reads=[pb[bank]], writes=[buf("R4")])

        for kb in range(KB):
            dma(wbufA[:, kb * D:(kb + 1) * D], V_scr[kb], reads=[buf("vscr")],
                writes=[buf("wA0")])
        for dt in range(8):
            for kb in range(KB):
                top(lambda e, dt=dt, kb=kb: e.matmul(
                    out=psum[dt][:, :],
                    lhsT=wbufA[:, kb * D + dt * 128: kb * D + dt * 128 + 128],
                    rhs=Gf[:, ts(kb, CH)], start=(kb == 0), stop=False),
                    reads=[buf("wA0"), buf("R5")], writes=[pb[dt]])
        for q8 in range(8):
            hbase = (q8 % 2) * 4096
            wb = buf(f"wA{q8 % 2}")
            dma(wbufA[:, hbase:hbase + 4096], W2_scr[q8], reads=[buf("w2scr")],
                writes=[wb], q="scalar")
            for j in range(4):
                kh = q8 * 4 + j
                for dt in range(8):
                    top(lambda e, j=j, dt=dt, kh=kh, hbase=hbase: e.matmul(
                        out=psum[dt][:, :],
                        lhsT=wbufA[:, hbase + j * 1024 + dt * 128: hbase + j * 1024 + dt * 128 + 128],
                        rhs=g1[:, ts(kh, CH)], start=False, stop=(kh == KH - 1)),
                        reads=[wb, buf("R4")], writes=[pb[dt]])
        for dt in range(8):
            sop(lambda e, dt=dt: e.activation(out=t1f[:], in_=psum[dt][:, :], func=AF.Identity,
                                              scale=gb_col[:, dt:dt + 1]),
                reads=[pb[dt], buf("gb")], writes=[buf("yb")])
            transpose_to(lambda k, dt=dt: x2tok[:, k * D + dt * 128: k * D + dt * 128 + 128],
                         buf("R1"), t1f, buf("yb"), ident32, buf("id"), NTT,
                         banks=(dt,))
        for tt in range(NTT):
            vop(lambda e, tt=tt: e.tensor_tensor(
                out=x2tok[:, ts(tt, D)], in0=x2tok[:, ts(tt, D)], in1=X1tok[:, ts(tt, D)],
                op=ALU.add),
                reads=[buf("R2")], writes=[buf("R1")], w=1024)
            ob = buf(f"out{c}_{tt}")
            dma(OUT[slice(t0 + tt * 128, t0 + tt * 128 + 128), :], x2tok[:, ts(tt, D)],
                reads=[buf("R1")], writes=[ob], q="gpsimd")
            out_bufs.append(ob)

    P.op("gpsimd", lambda e: e.nop(), reads=out_bufs)

    with nc.Block() as block:
        P.emit(block)
    es.close()
    return nc


_NC_CACHE = {}


def get_program(debug_taps=()):
    key = tuple(sorted(debug_taps))
    if key not in _NC_CACHE:
        _NC_CACHE[key] = build_program(debug_taps=key)
    return _NC_CACHE[key]


def make_in_map(inputs, b):
    return {
        "x": np.ascontiguousarray(inputs["L0_attn_x"][b]),
        "ln_w": inputs["L0_attn_ln_w"],
        "ln_b": inputs["L0_attn_ln_b"],
        "W_in": inputs["L0_attn_W_in"],
        "b_in": inputs["L0_attn_b_in"],
        "conv_w": np.ascontiguousarray(inputs["L0_attn_conv_w"].reshape(DI, 4)),
        "conv_b": inputs["L0_attn_conv_b"],
        "A_log": inputs["L0_attn_A_log"],
        "Dvec": inputs["L0_attn_D"],
        "W_x": inputs["L0_attn_W_x"],
        "b_x": inputs["L0_attn_b_x"],
        "W_out": inputs["L0_attn_W_out"],
        "b_out": inputs["L0_attn_b_out"],
        "W1": inputs["L0_ffn_W1"],
        "W2": inputs["L0_ffn_W2"],
        "ln_g": inputs["L0_ffn_ln_g"],
        "ln_b4": inputs["L0_ffn_ln_b"],
        "router_W": inputs["L0_ffn_router_W"],
        "router_b": inputs["L0_ffn_router_b"],
        "raw_U": inputs["L0_ffn_raw_U"],
        "raw_V": inputs["L0_ffn_raw_V"],
        "gamma": np.ascontiguousarray(inputs["L0_ffn_gamma"].reshape(D)),
        "c_row": np.ascontiguousarray(inputs["L0_c"][b]),
        "wmod": inputs["L0_wmod"],
    }


def kernel(**inputs):
    inputs = {k: np.ascontiguousarray(np.asarray(v, dtype=np.float32))
              for k, v in inputs.items()}
    nc = get_program()
    in_maps = [make_in_map(inputs, b) for b in range(8)]
    res = run_bass_kernel_spmd(nc, in_maps, core_ids=list(range(8)))
    out = np.stack([np.asarray(res.results[b]["out"]) for b in range(8)], axis=0)
    return out


# version 05
# speedup vs baseline: 1.2959x; 1.0422x over previous
import numpy as np
from contextlib import ExitStack

import concourse.bass as bass
import concourse.mybir as mybir
from concourse.bass_utils import run_bass_kernel_spmd

F32 = mybir.dt.float32
BF16 = mybir.dt.bfloat16
AF = mybir.ActivationFunctionType
ALU = mybir.AluOpType
AX = mybir.AxisListType.X

D = 1024
DI = 2048
T = 4096
NS = 16
NB = 512
HF = 4096
CH = 512
NCH = T // CH
NTT = CH // 128
KD = D // 128
KI = DI // 128
KH = HF // 128
KB = NB // 128
LN_EPS = 1e-5
TAU = 10.0
FEPS = 1e-6

ENGS = ("sync", "gpsimd", "vector", "scalar", "tensor")
GAP_WIDE = 1
GAP_NARROW = 4


def ts(i, n=128):
    return slice(i * n, (i + 1) * n)


class Buf:
    __slots__ = ("writer", "readers", "name")

    def __init__(self, name=""):
        self.writer = None
        self.readers = []
        self.name = name


class Prog:
    def __init__(self, nc, es):
        self.nc = nc
        self.ops = {e: [] for e in ENGS}
        self.sems = {e: es.enter_context(nc.semaphore(f"sem_{e}")) for e in ENGS}
        self.count = {e: 0 for e in ENGS}
        self.ord = {e: 0 for e in ENGS}
        self.waited = {e: set() for e in ENGS}

    def op(self, eng, fn, reads=(), writes=(), dma=False, w=0):
        rdeps = [b.writer for b in reads if b.writer is not None]
        wdeps = []
        for b in writes:
            if b.writer is not None:
                wdeps.append(b.writer)
            wdeps.extend(b.readers)
        inc = 16 if dma else 1
        pre = dict(self.count)
        self.count[eng] += inc
        self.ord[eng] += 1
        tok = (eng, self.count[eng], dma, self.ord[eng], w)
        wl = {}
        for (e, v, pdma, pord, pw) in rdeps:
            if pdma:
                if v > wl.get(e, 0):
                    wl[e] = v
            elif e != eng:
                if v > wl.get(e, 0):
                    wl[e] = v
            else:
                dist = self.ord[eng] - pord
                if dist < GAP_WIDE or (dist < GAP_NARROW and pw < 256):
                    if v > wl.get(e, 0):
                        wl[e] = v
        for (e, v, pdma, pord, pw) in wdeps:
            if pdma:
                if v > wl.get(e, 0):
                    wl[e] = v
            elif e != eng:
                if v > wl.get(e, 0):
                    wl[e] = v
            else:
                dist = self.ord[eng] - pord
                if dist < GAP_WIDE or (dist < GAP_NARROW and pw < 256):
                    if v > wl.get(e, 0):
                        wl[e] = v
        self.ops[eng].append((sorted(wl.items()), fn, dma))
        for (e, v) in wl.items():
            self.waited[e].add(v)
        for b in reads:
            b.readers.append(tok)
        for b in writes:
            b.writer = tok
            b.readers = []
        return tok

    def emit(self, block):
        sems = self.sems
        waited = self.waited

        def make_body(eng_name, eng_ops):
            def body(eng):
                hwm = {x: 0 for x in ENGS}
                pend = 0
                val = 0
                for waits, fn, isdma in eng_ops:
                    for (se, sv) in waits:
                        if sv > hwm[se]:
                            eng.wait_ge(sems[se], sv)
                            hwm[se] = sv
                    ins = fn(eng)
                    if isdma:
                        val += 16
                        ins.then_inc(sems[eng_name], 16)
                    else:
                        val += 1
                        pend += 1
                        if val in waited[eng_name]:
                            if eng_name in ("vector", "scalar"):
                                eng.drain(fusable=True).then_inc(sems[eng_name], pend)
                            else:
                                ins.then_inc(sems[eng_name], pend)
                            pend = 0
            return body

        for e in ENGS:
            getattr(block, e)(make_body(e, self.ops[e]))


def build_program(debug_taps=()):
    nc = bass.Bass("TRN2", target_bir_lowering=False)
    es = ExitStack()

    def din(name, shape):
        return nc.dram_tensor(name, list(shape), F32, kind="ExternalInput").ap()

    X_in = din("x", (T, D))
    ln_w_d = din("ln_w", (D,))
    ln_b_d = din("ln_b", (D,))
    W_in_d = din("W_in", (D, 2 * DI))
    b_in_d = din("b_in", (2 * DI,))
    conv_w_d = din("conv_w", (DI, 4))
    conv_b_d = din("conv_b", (DI,))
    A_log_d = din("A_log", (DI, NS))
    D_vec_d = din("Dvec", (DI,))
    W_x_d = din("W_x", (DI, 2 * NS))
    b_x_d = din("b_x", (2 * NS,))
    W_out_d = din("W_out", (DI, D))
    b_out_d = din("b_out", (D,))
    W1_d = din("W1", (HF, D))
    W2_d = din("W2", (D, HF))
    ln_g_d = din("ln_g", (D,))
    ln_b4_d = din("ln_b4", (D,))
    router_W_d = din("router_W", (NB, D))
    router_b_d = din("router_b", (NB,))
    raw_U_d = din("raw_U", (NB, D))
    raw_V_d = din("raw_V", (NB, D))
    gamma_d = din("gamma", (D,))
    c_row_d = din("c_row", (D,))
    wmod_d = din("wmod", (D, 6 * D))

    OUT = nc.dram_tensor("out", [T, D], F32, kind="ExternalOutput").ap()

    taps = {}
    TAPS = {
        "tp_xconv": ([DI, T], BF16),
        "tp_ssm": ([32, T], BF16),
        "tp_ytot": ([DI, T], BF16),
        "tp_X1": ([T, D], F32),
        "tp_r": ([T, NB], F32),
        "tp_G": ([T, NB], F32),
        "tp_hff": ([D, T], BF16),
        "tp_xn2f": ([D, T], BF16),
        "tp_xmain": ([DI, T], BF16),
        "tp_gate": ([DI, T], BF16),
        "tp_modrow": ([6, D], F32),
        "tp_xn1": ([T, D], F32),
        "tp_h1": ([T, D], F32),
        "tp_xup": ([T, D], F32),
        "tp_H0": ([128, NS * CH], BF16),
        "tp_gar": ([128, D], BF16),
        "tp_gacol": ([128, KD], F32),
        "tp_prep": ([128, 6 * D], BF16),
        "tp_sel": ([32, 32 * 128], BF16),
        "tp_asb": ([128, KI * NS], F32),
        "tp_misc": ([128, 160], F32),
        "tp_brep": ([128, NS * CH], BF16),
        "tp_crep": ([128, NS * CH], BF16),
    }
    for name in debug_taps:
        shp, dt_ = TAPS[name]
        taps[name] = nc.dram_tensor(name, list(shp), dt_, kind="ExternalOutput").ap()

    Win_scr = nc.dram_tensor("Win_scr", [8, 128, KD * 512], BF16, kind="Internal").ap()
    Wout_scr = nc.dram_tensor("Wout_scr", [4, 128, KI * 256], BF16, kind="Internal").ap()
    W1_scr = nc.dram_tensor("W1_scr", [8, 128, KD * 512], BF16, kind="Internal").ap()
    W2_scr = nc.dram_tensor("W2_scr", [8, 128, 4 * 1024], BF16, kind="Internal").ap()
    V_scr = nc.dram_tensor("V_scr", [KB, 128, D], BF16, kind="Internal").ap()
    mod_scr = nc.dram_tensor("mod_scr", [2, D], F32, kind="Internal").ap()

    def asb(name, shape, dtype=F32):
        return es.enter_context(nc.sbuf_tensor(name, list(shape), dtype))

    ident32 = asb("ident32", [128, 128], F32)
    ident16 = asb("ident16", [128, 128], BF16)
    pidx = asb("pidx", [128, 128], F32)
    ones16 = asb("ones16", [1, 128], BF16)
    ones32 = asb("ones32", [1, 128], F32)
    SEL = asb("SEL", [32, 32 * 128], BF16)
    ca1_rep = asb("ca1_rep", [128, D], BF16)
    ga_rep = asb("ga_rep", [128, D], BF16)
    sa_rep = asb("sa_rep", [128, D], BF16)
    cb1_rep = asb("cb1_rep", [128, D], BF16)
    sb_rep = asb("sb_rep", [128, D], BF16)
    ga_col = asb("ga_col", [128, KD], F32)
    gb_col = asb("gb_col", [128, KD], F32)
    gabo_col = asb("gabo_col", [128, KD], F32)
    sgn8 = asb("sgn8", [128, KD], F32)
    lnw_col = asb("lnw_col", [128, KD], F32)
    lnb16 = asb("lnb16", [128, KD], BF16)
    lng_col = asb("lng_col", [128, KD], F32)
    lnb4_16 = asb("lnb4_16", [128, KD], BF16)
    bout_col = asb("bout_col", [128, KD], F32)
    btin_col = asb("btin_col", [128, 32], F32)
    bx_col = asb("bx_col", [32, 1], F32)
    btr_row = asb("btr_row", [1, NB], BF16)
    btr32 = asb("btr32", [1, NB], F32)
    A_sb = asb("A_sb", [128, KI * NS], F32)
    D_col = asb("D_col", [128, KI], F32)
    convw_col = asb("convw_col", [128, KI * 4], F32)
    convb_col = asb("convb_col", [128, KI], F32)
    carry = asb("carry", [128, KI * NS], BF16)
    ctail = asb("ctail", [128, KI * 3], BF16)
    silu_c = asb("silu_c", [128, KD], F32)
    csb = asb("csb", [128, KD], F32)
    Wx_sb = asb("Wx_sb", [128, KI * 32], BF16)
    uvn = asb("uvn", [128, 4], F32)
    lns = asb("lns", [128, 8], F32)
    tks = asb("tks", [128, 24], F32)
    srow = asb("srow", [1, 512], F32)
    constt = asb("constt", [128, 2], F32)
    ssm_bf = asb("ssm_bf", [32, CH], BF16)
    convacc = asb("convacc", [128, CH], F32)
    Zbuf = asb("Zbuf", [128, CH], BF16)
    yb = asb("yb", [128, CH], F32)
    rW_sb = asb("rW_sb", [128, KD * NB], BF16)
    U_sb = asb("U_sb", [128, KD * NB], BF16)

    x_tok = asb("x_tok", [128, NTT * D], F32)
    lnA = asb("lnA", [128, D], F32)
    lnB = asb("lnB", [128, D], F32)
    wbufA = asb("wbufA", [128, 2 * 4096], BF16)
    R1 = asb("R1", [128, KI * (3 + CH)], BF16)
    R2 = asb("R2", [128, KI * CH], BF16)
    R3 = asb("R3", [128, KI * CH], BF16)
    R4 = asb("R4", [128, 2 * NS * CH], BF16)
    R5 = asb("R5", [128, NS * CH], BF16)
    W_slab = asb("W_slab", [128, NS * CH], BF16)

    psum = [es.enter_context(nc.psum_tensor(f"ps{i}", [128, 512], F32)) for i in range(8)]

    lnsq = lnA
    gam_rep = lnA
    t1f = yb
    x_main = R1
    ytot = R1[:, 0:KI * CH]
    x2tok = R1.bitcast(F32)[:, 0:NTT * D]
    x_conv = R2
    X1tok = R2.bitcast(F32)[:, 0:NTT * D]
    gate_s = R3
    hff = R3[:, 0:KD * CH]
    xn4f = R3[:, KD * CH:2 * KD * CH]
    prep_stage = R4.bitcast(F32)[:, 0:4096]
    prep_bf = R4[:, 2 * 4096:3 * 4096]
    B_rep = R4[:, 0:NS * CH]
    C_rep = R4[:, NS * CH:]
    g1 = R4
    xn2f = R5[:, 0:KD * CH]
    H_slab = R5
    r_tok = R5.bitcast(F32)[:, 0:512]
    hlat = R5.bitcast(F32)[:, 512:1024]
    alphaf = R5.bitcast(F32)[:, 1024:1536]
    zap = R5.bitcast(F32)[:, 1536:2048]
    G_tok = R5.bitcast(F32)[:, 2048:2560]
    Gf = R5[:, 5120:5120 + KB * CH]

    P = Prog(nc, es)
    B = {}

    def buf(name):
        if name not in B:
            B[name] = Buf(name)
        return B[name]

    pb = [buf(f"ps{i}") for i in range(8)]

    def dma(dst, src, reads=(), writes=(), q="sync", slow=False):
        if slow:
            return P.op(q, lambda e, d=dst, s=src: e.dma_start(
                out=d, in_=s, allow_slow_non_contiguous=True),
                reads=list(reads), writes=list(writes), dma=True)
        return P.op(q, lambda e, d=dst, s=src: e.dma_start(out=d, in_=s),
                    reads=list(reads), writes=list(writes), dma=True)

    def vop(fn, reads=(), writes=(), w=0):
        return P.op("vector", fn, reads=list(reads), writes=list(writes), w=w)

    def sop(fn, reads=(), writes=(), w=0):
        return P.op("scalar", fn, reads=list(reads), writes=list(writes), w=w)

    def top(fn, reads=(), writes=()):
        return P.op("tensor", fn, reads=list(reads), writes=list(writes))

    def gop(fn, reads=(), writes=()):
        return P.op("gpsimd", fn, reads=list(reads), writes=list(writes))

    gop(lambda e: e.iota(ident32[:], pattern=[[1, 128]], base=0,
                         channel_multiplier=0, allow_small_or_imprecise_dtypes=True),
        writes=[buf("id")])
    gop(lambda e: e.iota(pidx[:], pattern=[[0, 128]], base=0,
                         channel_multiplier=1, allow_small_or_imprecise_dtypes=True),
        writes=[buf("pidx")])
    vop(lambda e: e.tensor_tensor(out=ident32[:], in0=ident32[:], in1=pidx[:], op=ALU.is_equal),
        reads=[buf("pidx")], writes=[buf("id")])
    vop(lambda e: e.tensor_copy(out=ident16[:], in_=ident32[:]),
        reads=[buf("id")], writes=[buf("id16")])
    sop(lambda e: e.memzero(ones32[:]), writes=[buf("o32")])
    sop(lambda e: e.add(out=ones32[:], in_=ones32[:], add=1.0),
        reads=[buf("o32")], writes=[buf("o32")])
    vop(lambda e: e.tensor_copy(out=ones16[:], in_=ones32[:]),
        reads=[buf("o32")], writes=[buf("o16")])
    vop(lambda e: e.memset(carry[:], 0.0), writes=[buf("carry")])
    vop(lambda e: e.memset(constt[:, 0:1], LN_EPS), writes=[buf("constt")])
    nc.const_aps.register(F32, LN_EPS, constt[:, 0:1])
    vop(lambda e: e.memset(SEL[:], 0.0), writes=[buf("SEL")])
    for n in range(32):
        dma(SEL[n:n + 1, ts(n)], ones16[0:1, :], reads=[buf("o16")], writes=[buf("SEL")],
            q="gpsimd")

    def colload(dst, src1d, bname, n=128):
        dma(dst, src1d.rearrange("(a b) -> b a", b=n), writes=[buf(bname)], slow=True)

    colload(csb[:, 0:KD], c_row_d, "csb")
    sop(lambda e: e.activation(out=silu_c[:], in_=csb[:, 0:KD], func=AF.Silu),
        reads=[buf("csb")], writes=[buf("siluc")])
    colload(lnw_col[:, 0:KD], ln_w_d, "lnw")
    colload(sgn8[:, 0:KD], ln_b_d, "sgn")
    vop(lambda e: e.tensor_copy(out=lnb16[:], in_=sgn8[:, 0:KD]),
        reads=[buf("sgn")], writes=[buf("lnb16")])
    colload(lng_col[:, 0:KD], ln_g_d, "lng")
    colload(sgn8[:, 0:KD], ln_b4_d, "sgn")
    vop(lambda e: e.tensor_copy(out=lnb4_16[:], in_=sgn8[:, 0:KD]),
        reads=[buf("sgn")], writes=[buf("lnb4_16")])
    colload(bout_col[:, 0:KD], b_out_d, "bout")
    colload(btin_col[:], b_in_d, "btin")
    colload(convb_col[:], conv_b_d, "convb")
    colload(D_col[:], D_vec_d, "Dcol")
    for i_ in range(KI):
        dma(convw_col[:, i_ * 4:(i_ + 1) * 4], conv_w_d[ts(i_), :], writes=[buf("convw")])
    dma(bx_col[:], b_x_d.rearrange("(a b) -> b a", b=32), writes=[buf("bx")], slow=True)

    for ti in range(KI):
        dma(A_sb[:, ts(ti, NS)], A_log_d[ts(ti), :], writes=[buf("A")])
    sop(lambda e: e.activation(out=A_sb[:], in_=A_sb[:], func=AF.Exp),
        reads=[buf("A")], writes=[buf("A")])
    vop(lambda e: e.tensor_scalar(out=A_sb[:], in0=A_sb[:], scalar1=-1.0, scalar2=None,
                                  op0=ALU.mult),
        reads=[buf("A")], writes=[buf("A")], w=256)
    sop(lambda e: e.activation(out=A_sb[:], in_=A_sb[:], func=AF.Exp),
        reads=[buf("A")], writes=[buf("A")])

    for h in range(2):
        dma(srow[:], gamma_d[ts(h, 512)].unsqueeze(0), writes=[buf("srow")])
        top(lambda e: e.matmul(out=psum[5][:, :], lhsT=ones32[0:1, :],
                               rhs=srow[:], start=True, stop=True),
            reads=[buf("srow"), buf("o32")], writes=[pb[5]])
        sop(lambda e, h=h: e.copy(out=gam_rep[:, ts(h, 512)], in_=psum[5][:, :]),
            reads=[pb[5]], writes=[buf("lnA")])

    def rep_target(j):
        return {0: (sa_rep, "sarep", 0.0), 1: (ca1_rep, "carep", 1.0),
                3: (sb_rep, "sbrep", 0.0), 4: (cb1_rep, "cbrep", 1.0)}.get(j)

    for nsl in range(12):
        j, h = nsl // 2, nsl % 2
        dma(prep_stage.rearrange("p (k n) -> p k n", k=KD),
            wmod_d[:, ts(nsl, 512)].rearrange("(k p) n -> p k n", p=128),
            writes=[buf("R4")])
        for k in range(KD):
            top(lambda e, k=k: e.matmul(
                out=psum[6][0:1, :], lhsT=silu_c[:, k:k + 1],
                rhs=prep_stage[:, ts(k, 512)],
                start=(k == 0), stop=(k == KD - 1)),
                reads=[buf("R4"), buf("siluc")], writes=[pb[6]])
        sop(lambda e: e.copy(out=srow[:], in_=psum[6][0:1, :]),
            reads=[pb[6]], writes=[buf("srow")])
        tgt = rep_target(j)
        if tgt is not None:
            dst, dbn, bias = tgt
            top(lambda e: e.matmul(out=psum[5][:, :], lhsT=ones32[0:1, :], rhs=srow[:],
                                   start=True, stop=True),
                reads=[buf("srow"), buf("o32")], writes=[pb[5]])
            if bias == 0.0:
                sop(lambda e, dst=dst, h=h: e.activation(
                    out=dst[:, ts(h, 512)], in_=psum[5][:, :], func=AF.Copy),
                    reads=[pb[5]], writes=[buf(dbn)])
            else:
                sop(lambda e, dst=dst, h=h: e.activation(
                    out=dst[:, ts(h, 512)], in_=psum[5][:, :], func=AF.Copy, bias=1.0),
                    reads=[pb[5], buf("constt")], writes=[buf(dbn)])
        elif j in (2, 5):
            dma(mod_scr[0 if j == 2 else 1, ts(h, 512)].unsqueeze(0), srow[:],
                reads=[buf("srow")], writes=[buf("modscr")], q="gpsimd")
        if "tp_modrow" in taps:
            dma(taps["tp_modrow"][j, ts(h, 512)].unsqueeze(0), srow[:],
                reads=[buf("srow")], writes=[buf("tapm")])

    def clamp(col, cb_):
        vop(lambda e: e.tensor_scalar(out=sgn8[:], in0=col[:], scalar1=0.0, scalar2=2.0,
                                      op0=ALU.is_ge, op1=ALU.mult),
            reads=[cb_], writes=[buf("sgn")])
        vop(lambda e: e.tensor_scalar(out=sgn8[:], in0=sgn8[:], scalar1=-1.0, scalar2=None,
                                      op0=ALU.add),
            reads=[buf("sgn")], writes=[buf("sgn")])
        vop(lambda e: e.tensor_tensor(out=col[:], in0=col[:], in1=sgn8[:], op=ALU.mult),
            reads=[buf("sgn"), cb_], writes=[cb_])
        vop(lambda e: e.tensor_scalar(out=col[:], in0=col[:], scalar1=0.25, scalar2=None,
                                      op0=ALU.max),
            reads=[cb_], writes=[cb_])
        vop(lambda e: e.tensor_tensor(out=col[:], in0=col[:], in1=sgn8[:], op=ALU.mult),
            reads=[buf("sgn"), cb_], writes=[cb_])

    for j in (0, 1):
        for h in range(2):
            dma(srow[:], mod_scr[j, ts(h, 512)].unsqueeze(0),
                reads=[buf("modscr")], writes=[buf("srow")])
            vop(lambda e: e.tensor_scalar(out=btr32[:], in0=srow[:], scalar1=0.0,
                                          scalar2=2.0, op0=ALU.is_ge, op1=ALU.mult),
                reads=[buf("srow")], writes=[buf("btr32")])
            vop(lambda e: e.tensor_scalar(out=btr32[:], in0=btr32[:], scalar1=-1.0,
                                          scalar2=None, op0=ALU.add),
                reads=[buf("sgnr")], writes=[buf("btr32")])
            vop(lambda e: e.tensor_tensor(out=srow[:], in0=srow[:], in1=btr32[:],
                                          op=ALU.mult),
                reads=[buf("btr32"), buf("srow")], writes=[buf("srow")])
            vop(lambda e: e.tensor_scalar(out=srow[:], in0=srow[:], scalar1=0.25,
                                          scalar2=None, op0=ALU.max),
                reads=[buf("srow")], writes=[buf("srow")])
            vop(lambda e: e.tensor_tensor(out=srow[:], in0=srow[:], in1=btr32[:],
                                          op=ALU.mult),
                reads=[buf("btr32"), buf("srow")], writes=[buf("srow")])
            dma(mod_scr[j, ts(h, 512)].unsqueeze(0), srow[:],
                reads=[buf("srow")], writes=[buf("modscr")], q="gpsimd")
            if j == 0:
                top(lambda e: e.matmul(out=psum[5][:, :], lhsT=ones32[0:1, :],
                                       rhs=srow[:], start=True, stop=True),
                    reads=[buf("srow"), buf("o32")], writes=[pb[5]])
                sop(lambda e, h=h: e.copy(out=ga_rep[:, ts(h, 512)], in_=psum[5][:, :]),
                    reads=[pb[5]], writes=[buf("garep")])
    dma(ga_col[:, 0:KD], mod_scr[0].rearrange("(a b) -> b a", b=128),
        reads=[buf("modscr")], writes=[buf("ga")], slow=True)
    dma(gb_col[:, 0:KD], mod_scr[1].rearrange("(a b) -> b a", b=128),
        reads=[buf("modscr")], writes=[buf("gb")], slow=True)
    vop(lambda e: e.tensor_tensor(out=gabo_col[:], in0=ga_col[:], in1=bout_col[:, 0:KD],
                                  op=ALU.mult),
        reads=[buf("ga"), buf("bout")], writes=[buf("gabo")])

    vop(lambda e: e.memset(prep_bf[:, 0:512], 0.0), writes=[buf("R4")])
    top(lambda e: e.matmul(out=psum[7][:, :], lhsT=prep_bf[:, 0:128],
                           rhs=prep_bf[:, 0:512], start=True, stop=False),
        reads=[buf("R4")], writes=[pb[7]])
    for k in range(KD):
        dma(prep_stage[:], W_in_d[ts(k), :], writes=[buf("R4")])
        vop(lambda e: e.tensor_copy(out=prep_bf[:], in_=prep_stage[:]),
            reads=[buf("R4")], writes=[buf("R4")], w=4096)
        for o in range(32):
            top(lambda e, k=k, o=o: e.matmul(
                out=psum[7][:, o:o + 1], lhsT=prep_bf[:, ts(o)],
                rhs=lnb16[:, k:k + 1],
                start=False, stop=(k == KD - 1)),
                reads=[buf("R4"), buf("lnb16")], writes=[pb[7]])
        vop(lambda e, k=k: e.tensor_scalar(out=prep_bf[:], in0=prep_stage[:],
                                           scalar1=lnw_col[:, k:k + 1], scalar2=None,
                                           op0=ALU.mult),
            reads=[buf("R4"), buf("lnw")], writes=[buf("R4")], w=4096)
        for p in range(8):
            dma(Win_scr[p, :, ts(k, 512)], prep_bf[:, ts(p, 512)],
                reads=[buf("R4")], writes=[buf("winscr")], q="gpsimd")
    vop(lambda e: e.tensor_tensor(out=btin_col[:], in0=btin_col[:], in1=psum[7][:, 0:32],
                                  op=ALU.add),
        reads=[pb[7], buf("btin")], writes=[buf("btin")])

    for i in range(KI):
        dma(prep_stage[:, 0:1024], W_out_d[ts(i), :], writes=[buf("R4")])
        vop(lambda e: e.tensor_copy(out=prep_bf[:, 0:1024], in_=prep_stage[:, 0:1024]),
            reads=[buf("R4")], writes=[buf("R4")])
        for g in range(4):
            dma(Wout_scr[g, :, ts(i, 256)], prep_bf[:, g * 256:(g + 1) * 256],
                reads=[buf("R4")], writes=[buf("woscr")], q="gpsimd")

    for i in range(KI):
        dma(prep_stage[:, 0:32], W_x_d[ts(i), :], writes=[buf("R4")])
        vop(lambda e, i=i: e.tensor_copy(out=Wx_sb[:, i * 32:(i + 1) * 32],
                                         in_=prep_stage[:, 0:32]),
            reads=[buf("R4")], writes=[buf("Wx")])

    for r in range(KH):
        dma(prep_stage[:, 0:D], W1_d[ts(r), :], writes=[buf("R4")])
        for cc in range(KD):
            bank = cc % 2
            top(lambda e, cc=cc, bank=bank: e.transpose(
                out=psum[bank][:, 0:128], in_=prep_stage[:, ts(cc)], identity=ident32[:]),
                reads=[buf("R4"), buf("id")], writes=[pb[bank]])
            sop(lambda e, cc=cc, bank=bank: e.copy(
                out=prep_bf[:, cc * 128: cc * 128 + 128], in_=psum[bank][:, 0:128]),
                reads=[pb[bank]], writes=[buf("R4")])
        for cc in range(KD):
            dma(W1_scr[(r * 128 % 4096) // 512, :,
                       cc * 512 + (r * 128) % 512: cc * 512 + (r * 128) % 512 + 128],
                prep_bf[:, cc * 128: cc * 128 + 128],
                reads=[buf("R4")], writes=[buf("w1scr")], q="gpsimd")

    for r in range(KD):
        dma(prep_stage[:], W2_d[ts(r), :], writes=[buf("R4")])
        for q8 in range(4):
            for jj in range(8):
                cc = q8 * 8 + jj
                bank = jj % 2
                top(lambda e, cc=cc, bank=bank: e.transpose(
                    out=psum[bank][:, 0:128], in_=prep_stage[:, ts(cc)],
                    identity=ident32[:]),
                    reads=[buf("R4"), buf("id")], writes=[pb[bank]])
                sop(lambda e, jj=jj, bank=bank: e.copy(
                    out=prep_bf[:, jj * 128: jj * 128 + 128], in_=psum[bank][:, 0:128]),
                    reads=[pb[bank]], writes=[buf("R4")])
            for jj in range(8):
                cc = q8 * 8 + jj
                dma(W2_scr[cc // 4, :, (cc % 4) * 1024 + r * 128:
                          (cc % 4) * 1024 + r * 128 + 128],
                    prep_bf[:, jj * 128: jj * 128 + 128],
                    reads=[buf("R4")], writes=[buf("w2scr")], q="gpsimd")

    for r in range(KB):
        dma(prep_stage[:, 0:D], router_W_d[ts(r), :], writes=[buf("R4")])
        for cc in range(KD):
            bank = cc % 2
            top(lambda e, cc=cc, bank=bank: e.transpose(
                out=psum[bank][:, 0:128], in_=prep_stage[:, ts(cc)], identity=ident32[:]),
                reads=[buf("R4"), buf("id")], writes=[pb[bank]])
            sop(lambda e, cc=cc, r=r, bank=bank: e.activation(
                out=rW_sb[:, cc * NB + r * 128: cc * NB + r * 128 + 128],
                in_=psum[bank][:, 0:128], func=AF.Identity, scale=lng_col[:, cc:cc + 1]),
                reads=[pb[bank], buf("lng")], writes=[buf("rW")])
    dma(btr32[:], router_b_d.unsqueeze(0), writes=[buf("btr32")])
    for cc in range(KD):
        top(lambda e, cc=cc: e.matmul(out=psum[6][0:1, :],
                                      lhsT=lnb4_16[:, cc:cc + 1],
                                      rhs=rW_sb[:, cc * NB:(cc + 1) * NB],
                                      start=(cc == 0), stop=(cc == KD - 1)),
            reads=[buf("rW"), buf("lnb4_16")], writes=[pb[6]])
    vop(lambda e: e.tensor_tensor(out=btr32[:], in0=btr32[:], in1=psum[6][0:1, :],
                                  op=ALU.add),
        reads=[pb[6], buf("btr32")], writes=[buf("btr32")])
    vop(lambda e: e.tensor_copy(out=btr_row[:], in_=btr32[:]),
        reads=[buf("btr32")], writes=[buf("btr")])

    for (srcd, isU) in ((raw_U_d, True), (raw_V_d, False)):
        for r in range(KB):
            dma(prep_stage[:, 0:D], srcd[ts(r), :], writes=[buf("R4")])
            sop(lambda e: e.activation(out=lnB[:], in_=prep_stage[:, 0:D], func=AF.Square),
                reads=[buf("R4")], writes=[buf("lnB")])
            vop(lambda e: e.reduce_sum(out=uvn[:, 0:1], in_=lnB[:], axis=AX),
                reads=[buf("lnB")], writes=[buf("uvn")])
            sop(lambda e: e.activation(out=uvn[:, 1:2], in_=uvn[:, 0:1], func=AF.Sqrt),
                reads=[buf("uvn")], writes=[buf("uvn")])
            vop(lambda e: e.tensor_scalar(out=uvn[:, 1:2], in0=uvn[:, 1:2], scalar1=FEPS,
                                          scalar2=None, op0=ALU.max),
                reads=[buf("uvn")], writes=[buf("uvn")])
            vop(lambda e: e.reciprocal(out=uvn[:, 2:3], in_=uvn[:, 1:2]),
                reads=[buf("uvn")], writes=[buf("uvn")])
            vop(lambda e: e.tensor_scalar(out=prep_stage[:, 0:D], in0=prep_stage[:, 0:D],
                                          scalar1=uvn[:, 2:3], scalar2=None, op0=ALU.mult),
                reads=[buf("uvn")], writes=[buf("R4")])
            if isU:
                for cc in range(KD):
                    bank = cc % 2
                    top(lambda e, cc=cc, bank=bank: e.transpose(
                        out=psum[bank][:, 0:128], in_=prep_stage[:, ts(cc)],
                        identity=ident32[:]),
                        reads=[buf("R4"), buf("id")], writes=[pb[bank]])
                    sop(lambda e, cc=cc, r=r, bank=bank: e.copy(
                        out=U_sb[:, cc * NB + r * 128: cc * NB + r * 128 + 128],
                        in_=psum[bank][:, 0:128]),
                        reads=[pb[bank]], writes=[buf("U")])
            else:
                vop(lambda e: e.tensor_tensor(
                    out=prep_bf[:, 0:D], in0=prep_stage[:, 0:D],
                    in1=gam_rep[:], op=ALU.mult),
                    reads=[buf("R4"), buf("lnA")], writes=[buf("R4")])
                dma(V_scr[r], prep_bf[:, 0:D], reads=[buf("R4")], writes=[buf("vscr")],
                    q="gpsimd")

    if "tp_gar" in taps:
        dma(taps["tp_gar"][:], ga_rep[:], reads=[buf("garep")], writes=[buf("tapgr")])
    if "tp_prep" in taps:
        for nm, tsr, bn in (("ca", ca1_rep, "carep"), ("sa", sa_rep, "sarep"),
                            ("cb", cb1_rep, "cbrep"), ("sb", sb_rep, "sbrep"),
                            ("gar", ga_rep, "garep")):
            off = {"ca": 0, "sa": 1, "cb": 2, "sb": 3, "gar": 4}[nm] * D
            dma(taps["tp_prep"][:, off:off + D], tsr[:], reads=[buf(bn)],
                writes=[buf("tapp")])
    if "tp_sel" in taps:
        dma(taps["tp_sel"][:], SEL[:], reads=[buf("SEL")], writes=[buf("taps")])
    if "tp_asb" in taps:
        dma(taps["tp_asb"][:], A_sb[:], reads=[buf("A")], writes=[buf("tapa")])
    if "tp_misc" in taps:
        for nm, tsr, bn, off, wdt in (
                ("ga", ga_col, "ga", 0, KD), ("gb", gb_col, "gb", 8, KD),
                ("gabo", gabo_col, "gabo", 16, KD), ("btin", btin_col, "btin", 24, 32),
                ("cw", convw_col, "convw", 56, KI * 4), ("cb2", convb_col, "convb", 120, KI),
                ("dc", D_col, "Dcol", 136, KI)):
            dma(taps["tp_misc"][:, off:off + wdt], tsr[:, 0:wdt], reads=[buf(bn)],
                writes=[buf("tapm2")])
    if "tp_gacol" in taps:
        dma(taps["tp_gacol"][:], ga_col[:], reads=[buf("ga")], writes=[buf("tapgc")])

    inv_d = 1.0 / D

    def ln_stats(src_ap, src_buf):
        sop(lambda e: e.activation(out=lnsq[:], in_=src_ap, func=AF.Square),
            reads=[src_buf], writes=[buf("lnA")])
        vop(lambda e: e.reduce_sum(out=lns[:, 0:1], in_=src_ap, axis=AX),
            reads=[src_buf], writes=[buf("lns")])
        vop(lambda e: e.reduce_sum(out=lns[:, 1:2], in_=lnsq[:], axis=AX),
            reads=[buf("lnA")], writes=[buf("lns")])
        vop(lambda e: e.tensor_scalar(out=lns[:, 2:3], in0=lns[:, 0:1], scalar1=inv_d,
                                      scalar2=None, op0=ALU.mult),
            reads=[buf("lns")], writes=[buf("lns")])
        vop(lambda e: e.tensor_scalar(out=lns[:, 3:4], in0=lns[:, 1:2], scalar1=inv_d,
                                      scalar2=None, op0=ALU.mult),
            reads=[buf("lns")], writes=[buf("lns")])
        vop(lambda e: e.tensor_scalar(out=lns[:, 4:5], in0=lns[:, 2:3],
                                      scalar1=lns[:, 2:3], scalar2=None, op0=ALU.mult),
            reads=[buf("lns")], writes=[buf("lns")])
        vop(lambda e: e.tensor_tensor(out=lns[:, 4:5], in0=lns[:, 3:4], in1=lns[:, 4:5],
                                      op=ALU.subtract),
            reads=[buf("lns")], writes=[buf("lns")])
        sop(lambda e: e.activation(out=lns[:, 4:5], in_=lns[:, 4:5], func=AF.Sqrt,
                                   bias=LN_EPS),
            reads=[buf("lns"), buf("constt")], writes=[buf("lns")])
        vop(lambda e: e.reciprocal(out=lns[:, 5:6], in_=lns[:, 4:5]),
            reads=[buf("lns")], writes=[buf("lns")])
        vop(lambda e: e.tensor_scalar(out=lns[:, 6:7], in0=lns[:, 2:3],
                                      scalar1=lns[:, 5:6], scalar2=-1.0,
                                      op0=ALU.mult, op1=ALU.mult),
            reads=[buf("lns")], writes=[buf("lns")])

    def ln_apply(dst_ap, dst_buf, src_ap, src_buf):
        vop(lambda e: e.tensor_scalar(out=dst_ap, in0=src_ap, scalar1=lns[:, 5:6],
                                      scalar2=lns[:, 6:7], op0=ALU.mult, op1=ALU.add),
            reads=[src_buf, buf("lns")], writes=[dst_buf], w=1024)

    def transpose_to(dst_ap_fn, dst_buf, src_ap, src_buf, ident, ident_buf, nk,
                     banks=(0, 1)):
        for k in range(nk):
            bank = banks[k % len(banks)]
            col = (k // len(banks)) % 4 if len(banks) == 1 else 0
            top(lambda e, k=k, bank=bank, col=col: e.transpose(
                out=psum[bank][:, col * 128: col * 128 + 128], in_=src_ap[:, ts(k)],
                identity=ident[:]),
                reads=[src_buf, ident_buf], writes=[pb[bank]])
            sop(lambda e, k=k, bank=bank, col=col: e.copy(
                out=dst_ap_fn(k), in_=psum[bank][:, col * 128: col * 128 + 128]),
                reads=[pb[bank]], writes=[dst_buf])

    out_bufs = []

    for c in range(NCH):
        t0 = c * CH
        for tt in range(NTT):
            dma(x_tok[:, ts(tt, D)], X_in[slice(t0 + tt * 128, t0 + (tt + 1) * 128), :],
                writes=[buf("x_tok")])

        for tt in range(NTT):
            xs = x_tok[:, ts(tt, D)]
            ln_stats(xs, buf("x_tok"))
            ln_apply(lnB[:], buf("lnB"), xs, buf("x_tok"))
            vop(lambda e: e.tensor_tensor(out=lnB[:], in0=lnB[:], in1=ca1_rep[:],
                                          op=ALU.mult),
                reads=[buf("carep")], writes=[buf("lnB")], w=1024)
            vop(lambda e: e.tensor_tensor(out=lnB[:], in0=lnB[:], in1=sa_rep[:],
                                          op=ALU.add),
                reads=[buf("sarep")], writes=[buf("lnB")], w=1024)
            if "tp_h1" in taps:
                dma(taps["tp_h1"][slice(t0 + tt * 128, t0 + tt * 128 + 128), :], lnB[:],
                    reads=[buf("lnB")], writes=[buf("tapg")])
            vop(lambda e: e.tensor_tensor(out=lnA[:], in0=lnB[:], in1=ga_rep[:],
                                          op=ALU.mult),
                reads=[buf("lnB"), buf("garep")], writes=[buf("lnA")], w=1024)
            vop(lambda e, tt=tt: e.tensor_tensor(out=x_tok[:, ts(tt, D)],
                                                 in0=x_tok[:, ts(tt, D)], in1=lnA[:],
                                                 op=ALU.add),
                reads=[buf("lnA")], writes=[buf("x_tok")], w=1024)
            if "tp_xup" in taps:
                dma(taps["tp_xup"][slice(t0 + tt * 128, t0 + tt * 128 + 128), :],
                    x_tok[:, ts(tt, D)], reads=[buf("x_tok")], writes=[buf("tapxu")])
            ln_stats(lnB[:], buf("lnB"))
            ln_apply(lnA[:], buf("lnA"), lnB[:], buf("lnB"))
            transpose_to(lambda k, tt=tt: xn2f[:, k * CH + tt * 128: k * CH + tt * 128 + 128],
                         buf("R5"), lnA, buf("lnA"), ident32, buf("id"), KD)
            if "tp_xn1" in taps:
                dma(taps["tp_xn1"][slice(t0 + tt * 128, t0 + tt * 128 + 128), :], lnA[:],
                    reads=[buf("lnA")], writes=[buf("tapf")])

        if "tp_xn2f" in taps:
            for k in range(KD):
                dma(taps["tp_xn2f"][ts(k), t0:t0 + CH], xn2f[:, ts(k, CH)],
                    reads=[buf("R5")], writes=[buf("tap5b")])
        for i in range(KI):
            if c == 0:
                vop(lambda e, i=i: e.memset(x_main[:, i * 515: i * 515 + 3], 0.0),
                    writes=[buf("R1")])
            else:
                vop(lambda e, i=i: e.tensor_copy(out=x_main[:, i * 515: i * 515 + 3],
                                                 in_=ctail[:, i * 3:(i + 1) * 3]),
                    reads=[buf("ctail")], writes=[buf("R1")])
        for og in range(8):
            hbase = (og % 2) * 4096
            wb = buf(f"wA{og % 2}")
            dma(wbufA[:, hbase:hbase + 4096], Win_scr[og], reads=[buf("winscr")],
                writes=[wb])
            for oi in range(4):
                o = og * 4 + oi
                bank = o % 4
                for k in range(KD):
                    top(lambda e, k=k, oi=oi, bank=bank, hbase=hbase: e.matmul(
                        out=psum[bank][:, :],
                        lhsT=wbufA[:, hbase + k * 512 + oi * 128: hbase + k * 512 + oi * 128 + 128],
                        rhs=xn2f[:, ts(k, CH)], start=(k == 0), stop=(k == KD - 1)),
                        reads=[wb, buf("R5")], writes=[pb[bank]])
                if o < 16:
                    sop(lambda e, o=o, bank=bank: e.activation(
                        out=x_main[:, o * 515 + 3: o * 515 + 3 + CH],
                        in_=psum[bank][:, :], func=AF.Identity, bias=btin_col[:, o:o + 1]),
                        reads=[pb[bank], buf("btin")], writes=[buf("R1")])
                else:
                    sop(lambda e, o=o, bank=bank: e.activation(
                        out=gate_s[:, (o - 16) * CH: (o - 15) * CH],
                        in_=psum[bank][:, :], func=AF.Silu, bias=btin_col[:, o:o + 1]),
                        reads=[pb[bank], buf("btin")], writes=[buf("R3")])

        for i in range(KI):
            base = i * 515
            vop(lambda e, base=base, i=i: e.tensor_scalar(
                out=convacc[:], in0=x_main[:, base: base + CH],
                scalar1=convw_col[:, i * 4: i * 4 + 1], scalar2=None, op0=ALU.mult),
                reads=[buf("R1"), buf("convw")], writes=[buf("cacc")], w=CH)
            for kk in range(1, 4):
                vop(lambda e, base=base, i=i, kk=kk: e.scalar_tensor_tensor(
                    out=convacc[:], in0=x_main[:, base + kk: base + kk + CH],
                    scalar=convw_col[:, i * 4 + kk: i * 4 + kk + 1], in1=convacc[:],
                    op0=ALU.mult, op1=ALU.add),
                    reads=[buf("R1"), buf("convw"), buf("cacc")], writes=[buf("cacc")], w=CH)
            vop(lambda e, i=i: e.tensor_copy(out=ctail[:, i * 3:(i + 1) * 3],
                                             in_=x_main[:, i * 515 + CH: i * 515 + CH + 3]),
                reads=[buf("R1")], writes=[buf("ctail")])
            sop(lambda e, i=i: e.activation(out=x_conv[:, ts(i, CH)], in_=convacc[:],
                                            func=AF.Silu, bias=convb_col[:, i:i + 1]),
                reads=[buf("cacc"), buf("convb")], writes=[buf("R2")])

        if "tp_xmain" in taps:
            for i in range(KI):
                dma(taps["tp_xmain"][ts(i), t0:t0 + CH],
                    x_main[:, i * 515 + 3: i * 515 + 3 + CH],
                    reads=[buf("R1")], writes=[buf("tap8")])
        if "tp_gate" in taps:
            for i in range(KI):
                dma(taps["tp_gate"][ts(i), t0:t0 + CH], gate_s[:, ts(i, CH)],
                    reads=[buf("R3")], writes=[buf("tap9")])
        for i in range(KI):
            top(lambda e, i=i: e.matmul(out=psum[4][0:32, :],
                                        lhsT=Wx_sb[:, i * 32:(i + 1) * 32],
                                        rhs=x_conv[:, ts(i, CH)],
                                        start=(i == 0), stop=(i == KI - 1)),
                reads=[buf("Wx"), buf("R2")], writes=[pb[4]])
        sop(lambda e: e.activation(out=ssm_bf[:], in_=psum[4][0:32, :], func=AF.Identity,
                                   bias=bx_col[:]),
            reads=[pb[4], buf("bx")], writes=[buf("ssm")])
        if "tp_ssm" in taps:
            dma(taps["tp_ssm"][:, t0:t0 + CH], ssm_bf[:], reads=[buf("ssm")],
                writes=[buf("tap1")])

        for n in range(2 * NS):
            bank = 5 + (n % 2)
            top(lambda e, n=n, bank=bank: e.matmul(
                out=psum[bank][:, :], lhsT=SEL[:, ts(n)], rhs=ssm_bf[:, :],
                start=True, stop=True),
                reads=[buf("ssm"), buf("SEL")], writes=[pb[bank]])
            dstt = B_rep if n < NS else C_rep
            nn = n % NS
            sop(lambda e, nn=nn, bank=bank, dstt=dstt: e.copy(
                out=dstt[:, ts(nn, CH)], in_=psum[bank][:, :]),
                reads=[pb[bank]], writes=[buf("R4")])

        if c == 0 and "tp_brep" in taps:
            dma(taps["tp_brep"][:], B_rep[:], reads=[buf("R4")], writes=[buf("tapbr")])
        if c == 0 and "tp_crep" in taps:
            dma(taps["tp_crep"][:], C_rep[:], reads=[buf("R4")], writes=[buf("tapcr")])
        for i in range(KI):
            for n in range(NS):
                vop(lambda e, i=i, n=n: e.tensor_tensor(
                    out=Zbuf[:], in0=x_conv[:, ts(i, CH)], in1=B_rep[:, ts(n, CH)],
                    op=ALU.mult),
                    reads=[buf("R2"), buf("R4")], writes=[buf("Z")], w=CH)
                vop(lambda e, i=i, n=n: e.tensor_tensor_scan(
                    out=H_slab[:, ts(n, CH)],
                    data0=A_sb[:, i * NS + n: i * NS + n + 1].to_broadcast([128, CH]),
                    data1=Zbuf[:], initial=carry[:, i * NS + n: i * NS + n + 1],
                    op0=ALU.mult, op1=ALU.add),
                    reads=[buf("Z"), buf("A"), buf("carry")], writes=[buf("R5")], w=CH)
            if c == 0 and i == 0 and "tp_H0" in taps:
                dma(taps["tp_H0"][:], H_slab[:], reads=[buf("R5")], writes=[buf("tapH")])
            vop(lambda e, i=i: e.tensor_copy(out=carry[:, ts(i, NS)],
                                             in_=H_slab[:, CH - 1::CH]),
                reads=[buf("R5")], writes=[buf("carry")])
            vop(lambda e: e.tensor_tensor(out=W_slab[:], in0=H_slab[:], in1=C_rep[:],
                                          op=ALU.mult),
                reads=[buf("R5"), buf("R4")], writes=[buf("Wsl")], w=NS * CH)
            vop(lambda e: e.tensor_tensor(out=W_slab[:, 0:8 * CH], in0=W_slab[:, 0:8 * CH],
                                          in1=W_slab[:, 8 * CH:], op=ALU.add),
                writes=[buf("Wsl")], w=8 * CH)
            vop(lambda e: e.tensor_tensor(out=W_slab[:, 0:4 * CH], in0=W_slab[:, 0:4 * CH],
                                          in1=W_slab[:, 4 * CH:8 * CH], op=ALU.add),
                writes=[buf("Wsl")], w=4 * CH)
            vop(lambda e: e.tensor_tensor(out=W_slab[:, 0:2 * CH], in0=W_slab[:, 0:2 * CH],
                                          in1=W_slab[:, 2 * CH:4 * CH], op=ALU.add),
                writes=[buf("Wsl")], w=2 * CH)
            vop(lambda e: e.tensor_tensor(out=yb[:], in0=W_slab[:, 0:CH],
                                          in1=W_slab[:, CH:2 * CH], op=ALU.add),
                reads=[buf("Wsl")], writes=[buf("yb")], w=CH)
            vop(lambda e, i=i: e.scalar_tensor_tensor(
                out=convacc[:], in0=x_conv[:, ts(i, CH)], scalar=D_col[:, i:i + 1],
                in1=yb[:], op0=ALU.mult, op1=ALU.add),
                reads=[buf("R2"), buf("Dcol"), buf("yb")], writes=[buf("cacc")], w=CH)
            vop(lambda e, i=i: e.tensor_tensor(out=ytot[:, ts(i, CH)], in0=convacc[:],
                                               in1=gate_s[:, ts(i, CH)], op=ALU.mult),
                reads=[buf("cacc"), buf("R3")], writes=[buf("R1")], w=CH)
        if "tp_xconv" in taps:
            for i in range(KI):
                dma(taps["tp_xconv"][ts(i), t0:t0 + CH], x_conv[:, ts(i, CH)],
                    reads=[buf("R2")], writes=[buf("tap2")])
        if "tp_ytot" in taps:
            for i in range(KI):
                dma(taps["tp_ytot"][ts(i), t0:t0 + CH], ytot[:, ts(i, CH)],
                    reads=[buf("R1")], writes=[buf("tap3")])

        for g in range(4):
            hbase = (g % 2) * 4096
            wb = buf(f"wA{g % 2}")
            dma(wbufA[:, hbase:hbase + KI * 256], Wout_scr[g], reads=[buf("woscr")],
                writes=[wb])
            for dh in range(2):
                dt = g * 2 + dh
                bank = dt % 4
                for i in range(KI):
                    top(lambda e, i=i, dh=dh, bank=bank, hbase=hbase: e.matmul(
                        out=psum[bank][:, :],
                        lhsT=wbufA[:, hbase + i * 256 + dh * 128: hbase + i * 256 + dh * 128 + 128],
                        rhs=ytot[:, ts(i, CH)], start=(i == 0), stop=(i == KI - 1)),
                        reads=[wb, buf("R1")], writes=[pb[bank]])
                sop(lambda e, dt=dt, bank=bank: e.activation(
                    out=t1f[:], in_=psum[bank][:, :], func=AF.Identity,
                    scale=ga_col[:, dt:dt + 1], bias=gabo_col[:, dt:dt + 1]),
                    reads=[pb[bank], buf("ga"), buf("gabo")], writes=[buf("yb")])
                transpose_to(lambda k, dt=dt: X1tok[:, k * D + dt * 128:
                                                    k * D + dt * 128 + 128],
                             buf("R2"), t1f, buf("yb"), ident32, buf("id"), NTT)
        for tt in range(NTT):
            vop(lambda e, tt=tt: e.tensor_tensor(
                out=X1tok[:, ts(tt, D)], in0=X1tok[:, ts(tt, D)], in1=x_tok[:, ts(tt, D)],
                op=ALU.add),
                reads=[buf("x_tok")], writes=[buf("R2")], w=1024)
        if "tp_X1" in taps:
            for tt in range(NTT):
                dma(taps["tp_X1"][slice(t0 + tt * 128, t0 + tt * 128 + 128), :],
                    X1tok[:, ts(tt, D)], reads=[buf("R2")], writes=[buf("tap4")])

        for tt in range(NTT):
            xs = X1tok[:, ts(tt, D)]
            ln_stats(xs, buf("R2"))
            ln_apply(lnB[:], buf("lnB"), xs, buf("R2"))
            vop(lambda e: e.tensor_tensor(out=lnB[:], in0=lnB[:], in1=cb1_rep[:],
                                          op=ALU.mult),
                reads=[buf("cbrep")], writes=[buf("lnB")], w=1024)
            vop(lambda e: e.tensor_tensor(out=lnB[:], in0=lnB[:], in1=sb_rep[:],
                                          op=ALU.add),
                reads=[buf("sbrep")], writes=[buf("lnB")], w=1024)
            transpose_to(lambda k, tt=tt: hff[:, k * CH + tt * 128: k * CH + tt * 128 + 128],
                         buf("R3"), lnB, buf("lnB"), ident32, buf("id"), KD)
            ln_stats(lnB[:], buf("lnB"))
            ln_apply(lnA[:], buf("lnA"), lnB[:], buf("lnB"))
            transpose_to(lambda k, tt=tt: xn4f[:, k * CH + tt * 128: k * CH + tt * 128 + 128],
                         buf("R3"), lnA, buf("lnA"), ident32, buf("id"), KD)
        if "tp_hff" in taps:
            for k in range(KD):
                dma(taps["tp_hff"][ts(k), t0:t0 + CH], hff[:, ts(k, CH)],
                    reads=[buf("R3")], writes=[buf("tap5")])
        for tt in range(NTT):
            for k in range(KD):
                top(lambda e, k=k, tt=tt: e.matmul(
                    out=psum[4][:, :],
                    lhsT=xn4f[:, k * CH + tt * 128: k * CH + tt * 128 + 128],
                    rhs=rW_sb[:, k * NB:(k + 1) * NB], start=(k == 0), stop=False),
                    reads=[buf("R3"), buf("rW")], writes=[pb[4]])
            top(lambda e: e.matmul(out=psum[4][:, :], lhsT=ones16[0:1, :], rhs=btr_row[:],
                                   start=False, stop=True),
                reads=[buf("btr"), buf("o16")], writes=[pb[4]])
            sop(lambda e: e.copy(out=r_tok[:], in_=psum[4][:, :]),
                reads=[pb[4]], writes=[buf("R5")])
            for k in range(KD):
                top(lambda e, k=k, tt=tt: e.matmul(
                    out=psum[5][:, :],
                    lhsT=hff[:, k * CH + tt * 128: k * CH + tt * 128 + 128],
                    rhs=U_sb[:, k * NB:(k + 1) * NB], start=(k == 0), stop=(k == KD - 1)),
                    reads=[buf("R3"), buf("U")], writes=[pb[5]])
            sop(lambda e: e.copy(out=hlat[:], in_=psum[5][:, :]),
                reads=[pb[5]], writes=[buf("R5")])
            vop(lambda e: e.max(out=tks[:, 0:8], in_=r_tok[:]),
                reads=[buf("R5")], writes=[buf("tks")])
            vop(lambda e: e.tensor_scalar(out=alphaf[:], in0=r_tok[:], scalar1=TAU,
                                          scalar2=-TAU, op0=ALU.min, op1=ALU.max),
                writes=[buf("R5")], w=512)
            sop(lambda e: e.activation(out=alphaf[:], in_=alphaf[:], func=AF.Exp),
                reads=[buf("R5")], writes=[buf("R5")], w=512)
            sop(lambda e: e.activation(out=alphaf[:], in_=alphaf[:], func=AF.Ln, bias=1.0),
                reads=[buf("R5"), buf("constt")], writes=[buf("R5")], w=512)
            vop(lambda e: e.match_replace(out=zap[:], in_to_replace=tks[:, 0:8],
                                          in_values=r_tok[:], imm_value=-1e30),
                reads=[buf("tks")], writes=[buf("R5")])
            vop(lambda e: e.tensor_scalar(out=tks[:, 8:16], in0=tks[:, 0:8], scalar1=TAU,
                                          scalar2=-TAU, op0=ALU.min, op1=ALU.max),
                reads=[buf("tks")], writes=[buf("tks")])
            sop(lambda e: e.activation(out=tks[:, 8:16], in_=tks[:, 8:16], func=AF.Exp),
                reads=[buf("tks")], writes=[buf("tks")])
            sop(lambda e: e.activation(out=tks[:, 8:16], in_=tks[:, 8:16], func=AF.Ln,
                                       bias=1.0),
                reads=[buf("tks"), buf("constt")], writes=[buf("tks")])
            vop(lambda e: e.reduce_sum(out=tks[:, 16:17], in_=tks[:, 8:16], axis=AX),
                reads=[buf("tks")], writes=[buf("tks")])
            sop(lambda e: e.activation(out=tks[:, 17:18], in_=tks[:, 16:17], func=AF.Tanh),
                reads=[buf("tks")], writes=[buf("tks")])
            vop(lambda e: e.tensor_scalar(out=tks[:, 18:19], in0=tks[:, 16:17],
                                          scalar1=FEPS, scalar2=None, op0=ALU.add),
                reads=[buf("tks")], writes=[buf("tks")])
            vop(lambda e: e.reciprocal(out=tks[:, 19:20], in_=tks[:, 18:19]),
                reads=[buf("tks")], writes=[buf("tks")])
            vop(lambda e: e.tensor_scalar(out=tks[:, 20:21], in0=tks[:, 17:18],
                                          scalar1=tks[:, 19:20], scalar2=None,
                                          op0=ALU.mult),
                reads=[buf("tks")], writes=[buf("tks")])
            vop(lambda e: e.tensor_tensor(out=zap[:], in0=r_tok[:], in1=zap[:],
                                          op=ALU.not_equal),
                writes=[buf("R5")], w=512)
            vop(lambda e: e.tensor_tensor(out=zap[:], in0=zap[:], in1=alphaf[:],
                                          op=ALU.mult),
                writes=[buf("R5")], w=512)
            vop(lambda e: e.tensor_tensor(out=zap[:], in0=zap[:], in1=hlat[:],
                                          op=ALU.mult),
                writes=[buf("R5")], w=512)
            vop(lambda e: e.tensor_scalar(out=G_tok[:], in0=zap[:],
                                          scalar1=tks[:, 20:21], scalar2=None,
                                          op0=ALU.mult),
                reads=[buf("tks")], writes=[buf("R5")], w=512)
            if "tp_r" in taps:
                dma(taps["tp_r"][slice(t0 + tt * 128, t0 + tt * 128 + 128), :], r_tok[:],
                    reads=[buf("R5")], writes=[buf("tap6")])
            if "tp_G" in taps:
                dma(taps["tp_G"][slice(t0 + tt * 128, t0 + tt * 128 + 128), :], G_tok[:],
                    reads=[buf("R5")], writes=[buf("tap7")])
            transpose_to(lambda k, tt=tt: Gf[:, k * CH + tt * 128: k * CH + tt * 128 + 128],
                         buf("R5"), G_tok, buf("R5"), ident32, buf("id"), KB,
                         banks=(6, 7))

        for og in range(8):
            hbase = (og % 2) * 4096
            wb = buf(f"wA{og % 2}")
            dma(wbufA[:, hbase:hbase + 4096], W1_scr[og], reads=[buf("w1scr")],
                writes=[wb], q="scalar")
            for oi in range(4):
                o = og * 4 + oi
                bank = o % 4
                for k in range(KD):
                    top(lambda e, k=k, oi=oi, bank=bank, hbase=hbase: e.matmul(
                        out=psum[bank][:, :],
                        lhsT=wbufA[:, hbase + k * 512 + oi * 128: hbase + k * 512 + oi * 128 + 128],
                        rhs=hff[:, ts(k, CH)], start=(k == 0), stop=(k == KD - 1)),
                        reads=[wb, buf("R3")], writes=[pb[bank]])
                sop(lambda e, o=o, bank=bank: e.activation(
                    out=g1[:, ts(o, CH)], in_=psum[bank][:, :], func=AF.Gelu),
                    reads=[pb[bank]], writes=[buf("R4")])

        for kb in range(KB):
            dma(wbufA[:, kb * D:(kb + 1) * D], V_scr[kb], reads=[buf("vscr")],
                writes=[buf("wA0")])
        for dt in range(8):
            for kb in range(KB):
                top(lambda e, dt=dt, kb=kb: e.matmul(
                    out=psum[dt][:, :],
                    lhsT=wbufA[:, kb * D + dt * 128: kb * D + dt * 128 + 128],
                    rhs=Gf[:, ts(kb, CH)], start=(kb == 0), stop=False),
                    reads=[buf("wA0"), buf("R5")], writes=[pb[dt]])
        for q8 in range(8):
            hbase = (q8 % 2) * 4096
            wb = buf(f"wA{q8 % 2}")
            dma(wbufA[:, hbase:hbase + 4096], W2_scr[q8], reads=[buf("w2scr")],
                writes=[wb], q="scalar")
            for j in range(4):
                kh = q8 * 4 + j
                for dt in range(8):
                    top(lambda e, j=j, dt=dt, kh=kh, hbase=hbase: e.matmul(
                        out=psum[dt][:, :],
                        lhsT=wbufA[:, hbase + j * 1024 + dt * 128: hbase + j * 1024 + dt * 128 + 128],
                        rhs=g1[:, ts(kh, CH)], start=False, stop=(kh == KH - 1)),
                        reads=[wb, buf("R4")], writes=[pb[dt]])
        for dt in range(8):
            sop(lambda e, dt=dt: e.activation(out=t1f[:], in_=psum[dt][:, :], func=AF.Identity,
                                              scale=gb_col[:, dt:dt + 1]),
                reads=[pb[dt], buf("gb")], writes=[buf("yb")])
            transpose_to(lambda k, dt=dt: x2tok[:, k * D + dt * 128: k * D + dt * 128 + 128],
                         buf("R1"), t1f, buf("yb"), ident32, buf("id"), NTT,
                         banks=(dt,))
        for tt in range(NTT):
            vop(lambda e, tt=tt: e.tensor_tensor(
                out=x2tok[:, ts(tt, D)], in0=x2tok[:, ts(tt, D)], in1=X1tok[:, ts(tt, D)],
                op=ALU.add),
                reads=[buf("R2")], writes=[buf("R1")], w=1024)
            ob = buf(f"out{c}_{tt}")
            dma(OUT[slice(t0 + tt * 128, t0 + tt * 128 + 128), :], x2tok[:, ts(tt, D)],
                reads=[buf("R1")], writes=[ob], q="gpsimd")
            out_bufs.append(ob)

    P.op("gpsimd", lambda e: e.nop(), reads=out_bufs)

    with nc.Block() as block:
        P.emit(block)
    es.close()
    return nc


_NC_CACHE = {}


def get_program(debug_taps=()):
    key = tuple(sorted(debug_taps))
    if key not in _NC_CACHE:
        _NC_CACHE[key] = build_program(debug_taps=key)
    return _NC_CACHE[key]


def make_in_map(inputs, b):
    return {
        "x": np.ascontiguousarray(inputs["L0_attn_x"][b]),
        "ln_w": inputs["L0_attn_ln_w"],
        "ln_b": inputs["L0_attn_ln_b"],
        "W_in": inputs["L0_attn_W_in"],
        "b_in": inputs["L0_attn_b_in"],
        "conv_w": np.ascontiguousarray(inputs["L0_attn_conv_w"].reshape(DI, 4)),
        "conv_b": inputs["L0_attn_conv_b"],
        "A_log": inputs["L0_attn_A_log"],
        "Dvec": inputs["L0_attn_D"],
        "W_x": inputs["L0_attn_W_x"],
        "b_x": inputs["L0_attn_b_x"],
        "W_out": inputs["L0_attn_W_out"],
        "b_out": inputs["L0_attn_b_out"],
        "W1": inputs["L0_ffn_W1"],
        "W2": inputs["L0_ffn_W2"],
        "ln_g": inputs["L0_ffn_ln_g"],
        "ln_b4": inputs["L0_ffn_ln_b"],
        "router_W": inputs["L0_ffn_router_W"],
        "router_b": inputs["L0_ffn_router_b"],
        "raw_U": inputs["L0_ffn_raw_U"],
        "raw_V": inputs["L0_ffn_raw_V"],
        "gamma": np.ascontiguousarray(inputs["L0_ffn_gamma"].reshape(D)),
        "c_row": np.ascontiguousarray(inputs["L0_c"][b]),
        "wmod": inputs["L0_wmod"],
    }


def kernel(**inputs):
    inputs = {k: np.ascontiguousarray(np.asarray(v, dtype=np.float32))
              for k, v in inputs.items()}
    nc = get_program()
    in_maps = [make_in_map(inputs, b) for b in range(8)]
    res = run_bass_kernel_spmd(nc, in_maps, core_ids=list(range(8)))
    out = np.stack([np.asarray(res.results[b]["out"]) for b in range(8)], axis=0)
    return out
